# version 14
# baseline (speedup 1.0000x reference)
import os
import numpy as np
import ml_dtypes
import dataclasses as _dc
from contextlib import ExitStack

import concourse.bass as bass
import concourse.mybir as mybir
from concourse.bass_utils import run_bass_kernel_spmd

F32 = mybir.dt.float32
BF16 = mybir.dt.bfloat16
F16 = mybir.dt.float16
I16 = mybir.dt.int16
AF = mybir.ActivationFunctionType
ALU = mybir.AluOpType
AX = mybir.AxisListType

B, N, C = 4, 4096, 1024
TOK = 2048
NT = TOK // 128
DH = 4096
NCT = DH // 128
TH = 2052
NH = 8
HD = 64
NE = 8
EPS = 1e-5
PI = float(np.pi)

N_SUBLAYERS = int(os.environ.get("MK_NSUB", "6"))
SUB_ORDER = [int(x) for x in os.environ.get("MK_ORDER", "0,1,2,3,4,5").split(",")]
RG = [[0, 1], [2, 3], [4, 5], [6, 7]]


def _strided(apv, offset, dims):
    return _dc.replace(apv, offset=offset, ap=[list(apv.ap[0])] + [list(d) for d in dims])


def _apfull(apv, offset, dims):
    return _dc.replace(apv, offset=offset, ap=[list(d) for d in dims])


class Buf:
    __slots__ = ("t", "last_w", "readers")

    def __init__(self, t):
        self.t = t
        self.last_w = None
        self.readers = []

    def ap(self):
        return self.t[:]


DMA_POOL = {"q": 8, "gd": 4, "o": 2}


class K:
    def __init__(self, nc, es):
        self.nc = nc
        self.es = es
        self.eng = {"v": nc.vector, "s": nc.scalar, "t": nc.tensor,
                    "g": nc.gpsimd, "q": nc.sync}
        self.sem = {}
        self.cnt = {}
        self.step = {"v": 1, "s": 1, "t": 1, "g": 1, "q": 16, "gd": 16, "cc": 1, "o": 16}
        for name in list(self.step):
            if name in DMA_POOL:
                for i in range(DMA_POOL[name]):
                    nm = f"{name}{i}"
                    self.sem[nm] = es.enter_context(nc.semaphore("sem_" + nm))
                    self.cnt[nm] = 0
                    self.step[nm] = 16
            else:
                self.sem[name] = es.enter_context(nc.semaphore("sem_" + name))
                self.cnt[name] = 0
        self.rr = {name: 0 for name in DMA_POOL}
        self.wm = {e: {} for e in self.eng}
        self.pend = {e: [] for e in self.eng}

    @staticmethod
    def _e(ch):
        return {"gd": "g", "cc": "g", "o": "q"}.get(ch, ch)

    def wait(self, ch, tok):
        e = self._e(ch)
        s, v = tok
        if self.wm[e].get(s, -1) >= v:
            return
        self.eng[e].wait_ge(self.sem[s], v)
        self.wm[e][s] = v

    def op(self, ch, emit, reads=(), writes=(), inc=True):
        e = self._e(ch)
        if ch in DMA_POOL:
            i = self.rr[ch]
            self.rr[ch] = (i + 1) % DMA_POOL[ch]
            sch = f"{ch}{i}"
        else:
            sch = ch
        best = {}
        for b in reads:
            if b.last_w is not None:
                s, v = b.last_w
                best[s] = max(best.get(s, -1), v)
        for b in writes:
            toks = ([b.last_w] if b.last_w is not None else []) + b.readers
            for s, v in toks:
                best[s] = max(best.get(s, -1), v)
        if sch != ch and self.cnt[sch] > 0:
            best[sch] = max(best.get(sch, -1), self.cnt[sch])
        for s, v in best.items():
            self.wait(ch, (s, v))
        instr = emit(self.eng[e])
        if inc:
            self.cnt[sch] += self.step[sch]
            instr.then_inc(self.sem[sch], self.step[sch])
            tok = (sch, self.cnt[sch])
            for b in writes:
                b.last_w = tok
                b.readers = []
            for b in reads:
                b.readers.append(tok)
            for b in self.pend[e]:
                b.readers.append(tok)
            self.pend[e] = []
        else:
            for b in reads:
                self.pend[e].append(b)
        return instr

    def join(self):
        for e in self.eng:
            for ch in self.cnt:
                if self.cnt[ch] > 0:
                    self.wait(e, (ch, self.cnt[ch]))

    def sb(self, es, name, shape, dtype):
        return Buf(es.enter_context(self.nc.sbuf_tensor(name, list(shape), dtype)))

    def ps(self, es, name, shape, dtype=F32):
        return Buf(es.enter_context(self.nc.psum_tensor(name, list(shape), dtype)))


def _round_chain(k, ybuf, yap, z1buf, z1ap, outbuf, outap, clip1=False, cv_ch="v"):
    if cv_ch == "s":
        k.op("s", lambda e: e.activation(z1ap, yap, AF.Copy), reads=[ybuf], writes=[z1buf])
    else:
        k.op("v", lambda e: e.tensor_copy(z1ap, yap), reads=[ybuf], writes=[z1buf])
    k.op("v", lambda e: e.scalar_tensor_tensor(yap, z1ap, 0.5, yap, ALU.add, ALU.is_le),
         reads=[z1buf, ybuf], writes=[ybuf])
    if clip1:
        k.op("v", lambda e: e.scalar_tensor_tensor(yap, yap, -16384.0, z1ap, ALU.add, ALU.add),
             reads=[ybuf, z1buf], writes=[ybuf])
        k.op("v", lambda e: e.tensor_scalar(outap, yap, 1.0, -1.0, ALU.min, ALU.max),
             reads=[ybuf], writes=[outbuf])
    else:
        k.op("v", lambda e: e.scalar_tensor_tensor(outap, yap, -16384.0, z1ap, ALU.add, ALU.add),
             reads=[ybuf, z1buf], writes=[outbuf])


def build_program(nsub):
    nc = bass.Bass("TRN2", target_bir_lowering=False, debug=False)

    def din(name, shape, dtype=F32):
        return Buf(nc.dram_tensor(name, list(shape), dtype, kind="ExternalInput").ap())

    def dint(name, shape, dtype):
        return Buf(nc.dram_tensor(name, list(shape), dtype, kind="Internal").ap())

    x_own = din("x_own", [TOK, C])
    x_halo = din("x_halo", [2, C])
    w1t = din("w1t", [C, DH])
    w2t = din("w2t", [DH, C])
    convw = din("convw", [128, 3 * NCT])
    convb = din("convb", [128, NCT])
    qkvwt = din("qkvwt", [C, 3 * 512])
    projwt = din("projwt", [513, C])
    slopes8 = din("slopes8", [1, NH])
    l1qkvt = din("l1qkvt", [C, 3 * 512])
    l1wot = din("l1wot", [513, C])
    routerw = din("routerw", [C + 1, NE])
    expertw = din("expertw", [NE, C + 1, C])
    meta = din("meta", [1, 8])
    out_t = Buf(nc.dram_tensor("out", [TOK, C], F32, kind="ExternalOutput").ap())

    xres = dint("xres", [TOK, C], F32)
    wq1c = dint("wq1c", [8, 128, DH], BF16)
    wq2c = dint("wq2c", [NCT, 128, C], BF16)
    ag_in = dint("ag_in", [8, 128, TOK], BF16)
    ag_out = dint("ag_out", [2, 8, 128, TOK], BF16)
    rs_in = dint("rs_in", [4, 2, 512, C], F32)
    rs_out = dint("rs_out", [4, 512, C], F32)
    eh_in = dint("eh_in", [2, NCT, 128, 2], F32)
    eh_out = dint("eh_out", [NCT, 128, 2], F32)

    es = ExitStack()
    with es:
        k = K(nc, es)
        g = {}

        g["eye"] = k.sb(es, "g_eye", [128, 128], BF16)
        g["ones_row_f"] = k.sb(es, "g_onesrowf", [1, 128], F32)
        g["ones_col_f"] = k.sb(es, "g_onescolf", [128, 1], F32)

        g["meta"] = k.sb(es, "g_meta", [1, 8], F32)
        g["eps"] = k.sb(es, "g_eps", [128, 1], F32)
        g["ones_row_b16"] = k.sb(es, "g_onesrowb16", [1, 512], BF16)
        g["eyef"] = k.sb(es, "g_eyef", [128, 128], F32)
        k.op("g", lambda e: e.memset(g["eye"].ap(), 1.0), writes=[g["eye"]])
        k.op("g", lambda e: e.affine_select(g["eye"].ap(), g["eye"].ap(),
                                            pattern=[[-1, 128]], compare_op=ALU.is_equal,
                                            fill=0.0, base=0, channel_multiplier=1),
             reads=[g["eye"]], writes=[g["eye"]])
        k.op("g", lambda e: e.memset(g["ones_row_f"].ap(), 1.0), writes=[g["ones_row_f"]])
        k.op("g", lambda e: e.memset(g["ones_col_f"].ap(), 1.0), writes=[g["ones_col_f"]])

        k.op("g", lambda e: e.memset(g["eps"].ap(), EPS), writes=[g["eps"]])
        k.op("g", lambda e: e.memset(g["ones_row_b16"].ap(), 1.0), writes=[g["ones_row_b16"]])
        k.op("g", lambda e: e.memset(g["eyef"].ap(), 1.0), writes=[g["eyef"]])
        k.op("g", lambda e: e.affine_select(g["eyef"].ap(), g["eyef"].ap(),
                                            pattern=[[-1, 128]], compare_op=ALU.is_equal,
                                            fill=0.0, base=0, channel_multiplier=1),
             reads=[g["eyef"]], writes=[g["eyef"]])
        k.op("q", lambda e: e.dma_start(out=g["meta"].ap(), in_=meta.ap()),
             reads=[meta], writes=[g["meta"]])

        conv_cols = _load_conv(k, es, convw, convb)
        g["swinv"] = k.sb(es, "g_swinv", [128, 2], F32)
        with ExitStack() as pes:
            _prep_weights(k, pes, g, w1t, w2t, wq1c, wq2c)
        k.join()

        if True:
            for si in range(nsub):
                sub = SUB_ORDER[si]
                layer = 0 if sub < 3 else 1
                kind = ("ffn", "attn", "ffn")[sub % 3]
                xsrc = x_own if si == 0 else xres
                xdst = out_t if si == nsub - 1 else xres
                if kind == "ffn" and layer == 0:
                    _l0_ffn(k, g, sub, xsrc, xdst, x_halo, wq1c, wq2c,
                            conv_cols, eh_in, eh_out, first=(si == 0))
                elif kind == "ffn":
                    _moe(k, g, sub, xsrc, xdst, routerw, expertw)
                else:
                    _attention(k, g, sub, layer, xsrc, xdst,
                               qkvwt if layer == 0 else l1qkvt,
                               projwt if layer == 0 else l1wot,
                               slopes8, ag_in, ag_out, rs_in, rs_out)
                k.join()

        k.join()
        for i in range(DMA_POOL["o"]):
            nm = f"o{i}"
            if k.cnt[nm] > 0:
                k.eng["q"].wait_ge(k.sem[nm], k.cnt[nm])
    return nc


def _prep_weights(k, es, g, w1t, w2t, wq1c, wq2c):
    stage = [k.sb(es, f"wp_st{i}", [128, DH], F32) for i in range(6)]
    qstage = [k.sb(es, f"wp_q{i}", [128, 1024], BF16) for i in range(4)]
    z1 = [k.sb(es, f"wp_z1{i}", [128, 1024], I16) for i in range(4)]
    acc = k.sb(es, "wp_acc", [128, 8 + NCT], F32)
    tot = k.sb(es, "wp_tot", [128, 2], F32)
    m2 = k.sb(es, "wp_m2", [1, 2], F32)
    swrow = k.sb(es, "wp_swrow", [1, 2], F32)
    swc = k.sb(es, "wp_swc", [128, 2], F32)
    minv = k.sb(es, "wp_minv", [128, 2], F32)
    pcol = k.ps(es, "wp_pcol", [1, 2])
    pbc = k.ps(es, "wp_pbc", [128, 2])

    specs = [(w1t, 8, DH, wq1c, 0), (w2t, NCT, C, wq2c, 1)]
    for wt, npt, width, _, wi in specs:
        for i in range(npt):
            sbuf = stage[i % 6]
            col = i if wi == 0 else 8 + i
            hw = width // 2
            for h2, ch in ((0, "q"), (1, "gd")):
                k.op(ch, lambda e, wt=wt, i=i, hw=hw, h2=h2, sbuf=sbuf: e.dma_start(
                    out=sbuf.t[:, h2 * hw:(h2 + 1) * hw],
                    in_=wt.t[i * 128:(i + 1) * 128, h2 * hw:(h2 + 1) * hw]),
                    reads=[wt], writes=[sbuf])
            k.op("v", lambda e, sbuf=sbuf, col=col, width=width: e.reduce_sum(
                acc.t[:, col:col + 1], sbuf.t[:, 0:width], axis=AX.X,
                apply_absolute_value=True),
                reads=[sbuf], writes=[acc])
    k.op("v", lambda e: e.reduce_sum(tot.t[:, 0:1], acc.t[:, 0:8], axis=AX.X),
         reads=[acc], writes=[tot])
    k.op("v", lambda e: e.reduce_sum(tot.t[:, 1:2], acc.t[:, 8:8 + NCT], axis=AX.X),
         reads=[acc], writes=[tot])
    k.op("t", lambda e: e.matmul(pcol.ap(), g["ones_col_f"].ap(), tot.ap(),
                                 start=True, stop=True),
         reads=[g["ones_col_f"], tot], writes=[pcol])
    k.op("v", lambda e: e.tensor_scalar(m2.ap(), pcol.ap(), 1.0 / (DH * C), 1e-5,
                                        ALU.mult, ALU.max),
         reads=[pcol], writes=[m2])
    k.op("v", lambda e: e.reciprocal(swrow.ap(), m2.ap()), reads=[m2], writes=[swrow])
    k.op("t", lambda e: e.matmul(pbc.ap(), g["ones_row_f"].ap(), swrow.ap(),
                                 start=True, stop=True),
         reads=[g["ones_row_f"], swrow], writes=[pbc])
    k.op("v", lambda e: e.tensor_copy(swc.ap(), pbc.ap()), reads=[pbc], writes=[swc])
    k.op("t", lambda e: e.matmul(pbc.ap(), g["ones_row_f"].ap(), m2.ap(),
                                 start=True, stop=True),
         reads=[g["ones_row_f"], m2], writes=[pbc])
    k.op("v", lambda e: e.tensor_copy(minv.ap(), pbc.ap()), reads=[pbc], writes=[minv])
    k.op("v", lambda e: e.tensor_copy(g["swinv"].ap(), minv.ap()),
         reads=[minv], writes=[g["swinv"]])

    nstg = len(stage)
    sci = 0
    for wt, npt, width, cache, wi in specs:
        nsub_w = width // 1024
        for i in range(npt):
            for sc in range(nsub_w):
                sbuf = stage[sci % nstg]
                zb, qs = z1[sci % 4], qstage[sci % 4]
                ch = "q" if sci % 2 == 0 else "gd"
                c0 = sc * 1024
                k.op(ch, lambda e, wt=wt, i=i, c0=c0, sbuf=sbuf: e.dma_start(
                    out=sbuf.t[:, 0:1024],
                    in_=wt.t[i * 128:(i + 1) * 128, c0:c0 + 1024]),
                    reads=[wt], writes=[sbuf])
                k.op("s", lambda e, sbuf=sbuf, wi=wi: e.activation(
                    sbuf.t[:, 0:1024], sbuf.t[:, 0:1024], AF.Copy,
                    bias=16384.0, scale=swc.t[:, wi:wi + 1]),
                    reads=[sbuf, swc], writes=[sbuf])
                _round_chain(k, sbuf, sbuf.t[:, 0:1024], zb, zb.t[:, 0:1024],
                             qs, qs.t[:, 0:1024], clip1=True, cv_ch="s")
                k.op("gd" if sci % 2 == 0 else "q",
                     lambda e, cache=cache, i=i, c0=c0, qs=qs: e.dma_start(
                         out=cache.t[i, :, c0:c0 + 1024], in_=qs.t[:, 0:1024]),
                     reads=[qs], writes=[cache])
                sci += 1
    k.join()


def _load_conv(k, es, convw, convb):
    cw = k.sb(es, "g_convw", [128, 3 * NCT], F32)
    cb = k.sb(es, "g_convb", [128, NCT], F32)
    k.op("q", lambda e: e.dma_start(out=cw.ap(), in_=convw.ap()),
         reads=[convw], writes=[cw])
    k.op("q", lambda e: e.dma_start(out=cb.ap(), in_=convb.ap()),
         reads=[convb], writes=[cb])
    return {"w": cw, "b": cb}


def _ln_stats(k, cols, xb, P, quant):
    pass


def _l0_ffn(k, g, sub, xsrc, xdst, x_halo, wq1c, wq2c, conv_cols, eh_in, eh_out,
            first):
    nc = k.nc
    with ExitStack() as es:
        xqT = k.sb(es, f"s{sub}_xqT", [128, 8 * TH], BF16)
        hb = [k.sb(es, f"s{sub}_h{ct}", [128, TH], BF16) for ct in range(NCT)]
        st1rep = k.sb(es, f"s{sub}_st1rep", [128, TH], BF16)
        st2rep = k.sb(es, f"s{sub}_st2rep", [128, TH], BF16)
        stcol = k.sb(es, f"s{sub}_stcol", [128, 32], F32)
        mxcol = k.sb(es, f"s{sub}_mxcol", [128, 32], F32)
        strow = k.sb(es, f"s{sub}_strow", [32, 128], F32)
        strowf = k.sb(es, f"s{sub}_strowf", [1, 18 * 128], BF16)
        st2col = k.sb(es, f"s{sub}_st2col", [128, 32], F32)
        st2i05 = k.sb(es, f"s{sub}_st2i05", [128, NT], F32)
        w1b = [k.sb(es, f"s{sub}_w1b{i}", [128, 8 * 128], BF16) for i in range(2)]

        sw1b = g["swinv"]

        with ExitStack() as aes:
            ptb = [k.ps(aes, f"s{sub}_ptb{i}", [128, 128], BF16) for i in range(2)]
            xt = [k.sb(aes, f"s{sub}_xt{i}", [128, C], F32) for i in range(2)]
            ysc = [k.sb(aes, f"s{sub}_ysc{i}", [128, C], F32) for i in range(2)]
            z1 = [k.sb(aes, f"s{sub}_z1{i}", [128, C], I16) for i in range(2)]
            xqs = [k.sb(aes, f"s{sub}_xqs{i}", [128, C], BF16) for i in range(2)]
            sdump = k.sb(aes, f"s{sub}_sdump", [128, C], BF16)
            colsb = [k.sb(aes, f"s{sub}_cols{i}", [128, 12], F32) for i in range(2)]

            def ln_quant_tile(tt, P, src_ap, srcbuf, halo):
                i = tt % 2
                xb, yb, zb, qb, cols = xt[i], ysc[i], z1[i], xqs[i], colsb[i]
                k.op("q", lambda e: e.dma_start(out=xb.t[0:P, :], in_=src_ap),
                     reads=[srcbuf], writes=[xb])
                k.op("v", lambda e: e.reduce_sum(cols.t[0:P, 0:1], xb.t[0:P, :], axis=AX.X),
                     reads=[xb], writes=[cols])
                k.op("s", lambda e: e.activation(sdump.t[0:P, :], xb.t[0:P, :], AF.Square,
                                                 accum_out=cols.t[0:P, 1:2]),
                     reads=[xb], writes=[sdump, cols])
                k.op("v", lambda e: e.reduce_max(cols.t[0:P, 5:6], xb.t[0:P, :], axis=AX.X),
                     reads=[xb], writes=[cols])
                k.op("v", lambda e: e.tensor_reduce(cols.t[0:P, 6:7], xb.t[0:P, :],
                                                    AX.X, ALU.min),
                     reads=[xb], writes=[cols])
                k.op("v", lambda e: e.tensor_scalar(cols.t[0:P, 2:3], cols.t[0:P, 0:1],
                                                    1.0 / C, None, ALU.mult),
                     reads=[cols], writes=[cols])
                k.op("v", lambda e: e.tensor_mul(cols.t[0:P, 3:4], cols.t[0:P, 2:3],
                                                 cols.t[0:P, 2:3]),
                     reads=[cols], writes=[cols])
                k.op("v", lambda e: e.scalar_tensor_tensor(cols.t[0:P, 3:4],
                                                           cols.t[0:P, 1:2], 1.0 / C,
                                                           cols.t[0:P, 3:4],
                                                           ALU.mult, ALU.subtract),
                     reads=[cols], writes=[cols])
                k.op("s", lambda e: e.activation(cols.t[0:P, 4:5], cols.t[0:P, 3:4],
                                                 AF.Sqrt, bias=g["eps"].t[0:P, :]),
                     reads=[cols, g["eps"]], writes=[cols])
                k.op("v", lambda e: e.reciprocal(cols.t[0:P, 4:5], cols.t[0:P, 4:5]),
                     reads=[cols], writes=[cols])
                k.op("v", lambda e: e.tensor_sub(cols.t[0:P, 5:6], cols.t[0:P, 5:6],
                                                 cols.t[0:P, 2:3]),
                     reads=[cols], writes=[cols])
                k.op("v", lambda e: e.tensor_sub(cols.t[0:P, 6:7], cols.t[0:P, 2:3],
                                                 cols.t[0:P, 6:7]),
                     reads=[cols], writes=[cols])
                k.op("v", lambda e: e.tensor_max(cols.t[0:P, 7:8], cols.t[0:P, 5:6],
                                                 cols.t[0:P, 6:7]),
                     reads=[cols], writes=[cols])
                k.op("v", lambda e: e.tensor_mul(cols.t[0:P, 7:8], cols.t[0:P, 7:8],
                                                 cols.t[0:P, 4:5]),
                     reads=[cols], writes=[cols])
                k.op("v", lambda e: e.tensor_scalar(cols.t[0:P, 7:8], cols.t[0:P, 7:8],
                                                    1e-5, None, ALU.max),
                     reads=[cols], writes=[cols])
                k.op("v", lambda e: e.reciprocal(cols.t[0:P, 8:9], cols.t[0:P, 7:8]),
                     reads=[cols], writes=[cols])
                k.op("v", lambda e: e.tensor_scalar(cols.t[0:P, 8:9], cols.t[0:P, 8:9],
                                                    127.0, None, ALU.mult),
                     reads=[cols], writes=[cols])
                k.op("v", lambda e: e.tensor_mul(cols.t[0:P, 8:9], cols.t[0:P, 8:9],
                                                 cols.t[0:P, 4:5]),
                     reads=[cols], writes=[cols])
                k.op("v", lambda e: e.tensor_mul(cols.t[0:P, 9:10], cols.t[0:P, 2:3],
                                                 cols.t[0:P, 8:9]),
                     reads=[cols], writes=[cols])
                k.op("v", lambda e: e.tensor_scalar(cols.t[0:P, 9:10], cols.t[0:P, 9:10],
                                                    -1.0, 16384.0, ALU.mult, ALU.add),
                     reads=[cols], writes=[cols])
                k.op("v", lambda e: e.scalar_tensor_tensor(
                    cols.t[0:P, 10:11], cols.t[0:P, 7:8], 1.0 / 127.0,
                    sw1b.t[0:P, 0:1], ALU.mult, ALU.mult),
                    reads=[cols, sw1b], writes=[cols])
                k.op("s", lambda e: e.activation(yb.t[0:P, :], xb.t[0:P, :], AF.Identity,
                                                 bias=cols.t[0:P, 9:10],
                                                 scale=cols.t[0:P, 8:9]),
                     reads=[xb, cols], writes=[yb])
                _round_chain(k, yb, yb.t[0:P, :], zb, zb.t[0:P, :], qb, qb.t[0:P, :],
                             cv_ch="s")
                dstc = 17 if halo else tt
                k.op("v", lambda e: e.tensor_copy(stcol.t[0:P, dstc:dstc + 1],
                                                  cols.t[0:P, 10:11]),
                     reads=[cols], writes=[stcol])
                for j in range(8):
                    pb = ptb[j % 2]
                    k.op("t", lambda e, j=j: e.transpose(
                        pb.t[:, 0:P], qb.t[0:P, j * 128:(j + 1) * 128],
                        g["eye"].t[0:P, 0:P]),
                        reads=[qb, g["eye"]], writes=[pb])
                    if not halo:
                        k.op("v", lambda e, j=j, pb=pb: e.tensor_copy(
                            xqT.t[:, TH * j + 1 + tt * 128: TH * j + 1 + (tt + 1) * 128],
                            pb.t[:, 0:128]),
                            reads=[pb], writes=[xqT])
                    else:
                        k.op("v", lambda e, j=j, pb=pb: e.tensor_copy(
                            xqT.t[:, TH * j: TH * j + 1], pb.t[:, 0:1]),
                            reads=[pb], writes=[xqT])
                        k.op("v", lambda e, j=j, pb=pb: e.tensor_copy(
                            xqT.t[:, TH * j + 2049: TH * j + 2050], pb.t[:, 1:2]),
                            reads=[pb], writes=[xqT])

            for j in range(8):
                k.op("v", lambda e, j=j: e.memset(xqT.t[:, TH * j + 2050: TH * (j + 1)], 0.0),
                     writes=[xqT])
            k.op("v", lambda e: e.memset(stcol.ap(), 0.0), writes=[stcol])
            for tt in range(NT):
                ln_quant_tile(tt, 128, xsrc.t[tt * 128:(tt + 1) * 128, :], xsrc, False)
            if first:
                ln_quant_tile(16, 2, x_halo.ap(), x_halo, True)
            else:
                for j in range(8):
                    k.op("v", lambda e, j=j: e.memset(xqT.t[:, TH * j: TH * j + 1], 0.0),
                         writes=[xqT])
                    k.op("v", lambda e, j=j: e.memset(
                        xqT.t[:, TH * j + 2049: TH * j + 2050], 0.0),
                        writes=[xqT])
                k.op("v", lambda e: e.memset(stcol.t[:, 17:18], 0.0), writes=[stcol])

            prow = k.ps(aes, f"s{sub}_prow", [128, 128])
            pst = k.ps(aes, f"s{sub}_pst", [32, 128])
            k.op("t", lambda e: e.transpose(pst.ap(), stcol.ap(), g["eyef"].ap()),
                 reads=[stcol, g["eyef"]], writes=[pst])
            k.op("v", lambda e: e.tensor_copy(strow.ap(), pst.ap()),
                 reads=[pst], writes=[strow])
            k.op("gd", lambda e: e.dma_start(
                out=_strided(strowf.ap(), 0, [[128, 18], [1, 128]]),
                in_=strow.t[0:18, :]),
                reads=[strow], writes=[strowf])
            for tt in range(NT):
                k.op("t", lambda e, tt=tt: e.matmul(prow.ap(), g["ones_row_b16"].t[0:1, 0:128],
                                                    strowf.t[:, tt * 128:(tt + 1) * 128],
                                                    start=True, stop=True),
                     reads=[g["ones_row_b16"], strowf], writes=[prow])
                k.op("v", lambda e, tt=tt: e.tensor_copy(
                    st1rep.t[:, 1 + tt * 128:1 + (tt + 1) * 128], prow.ap()),
                    reads=[prow], writes=[st1rep])
            k.op("t", lambda e: e.matmul(prow.t[:, 0:2], g["ones_row_b16"].t[0:1, 0:128],
                                         strowf.t[:, 17 * 128:17 * 128 + 2],
                                         start=True, stop=True),
                 reads=[g["ones_row_b16"], strowf], writes=[prow])
            k.op("v", lambda e: e.tensor_copy(st1rep.t[:, 0:1], prow.t[:, 0:1]),
                 reads=[prow], writes=[st1rep])
            k.op("v", lambda e: e.tensor_copy(st1rep.t[:, 2049:2050], prow.t[:, 1:2]),
                 reads=[prow], writes=[st1rep])
            k.op("v", lambda e: e.memset(st1rep.t[:, 2050:TH], 0.0), writes=[st1rep])
            k.join()

        hrecv = None
        if not first:
            hrecv = k.sb(es, f"s{sub}_hrecv", [128, 2 * NCT], BF16)
            with ExitStack() as bes:
                pe = k.ps(bes, f"s{sub}_pedge", [128, 2])
                pfl = k.ps(bes, f"s{sub}_pfl", [128, 2])
                edge = k.sb(bes, f"s{sub}_edge", [128, 2 * NCT], F32)
                sedge = k.sb(bes, f"s{sub}_sedge", [128, 2 * NCT], F32)
                flg = k.sb(bes, f"s{sub}_flg", [128, 2], F32)
                k.op("t", lambda e: e.matmul(pfl.ap(), g["ones_row_f"].ap(),
                                             g["meta"].t[:, 0:2], start=True, stop=True),
                     reads=[g["ones_row_f"], g["meta"]], writes=[pfl])
                k.op("v", lambda e: e.tensor_copy(flg.ap(), pfl.ap()),
                     reads=[pfl], writes=[flg])
                for ct in range(NCT):
                    wb = w1b[ct % 2]
                    k.op("q", lambda e, ct=ct, wb=wb: e.dma_start(
                        out=_strided(wb.ap(), 0, [[128, 8], [1, 128]]),
                        in_=_apfull(wq1c.ap(), ct * 128,
                                    [[DH, 128], [128 * DH, 8], [1, 128]])),
                        reads=[wq1c], writes=[wb])
                    for ki in range(8):
                        rhs = _strided(xqT.ap(), TH * ki + 1, [[2047, 2]])
                        k.op("t", lambda e, ki=ki, wb=wb, rhs=rhs: e.matmul(
                            pe.ap(), wb.t[:, ki * 128:(ki + 1) * 128], rhs,
                            start=(ki == 0), stop=(ki == 7)),
                            reads=[wb, xqT], writes=[pe], inc=(ki == 7))
                    srep = _strided(st1rep.ap(), 1, [[2047, 2]])
                    k.op("v", lambda e, ct=ct, srep=srep: e.tensor_tensor(
                        edge.t[:, 2 * ct:2 * ct + 2], pe.ap(), srep, ALU.mult),
                        reads=[pe, st1rep], writes=[edge])
                ev = edge.t.rearrange("p (ct w) -> p ct w", w=2)
                sv = sedge.t.rearrange("p (ct w) -> p ct w", w=2)
                k.op("v", lambda e: e.memset(sv[:, :, 0], 0.0), writes=[sedge])
                k.op("v", lambda e: e.tensor_scalar(sv[:, :, 1], ev[:, :, 0],
                                                    flg.t[:, 1:2], None, ALU.mult),
                     reads=[edge, flg], writes=[sedge])
                k.op("q", lambda e: e.dma_start(
                    out=_apfull(eh_in.ap(), 0, [[2, 128], [256, NCT], [1, 2]]),
                in_=sedge.ap()),
                    reads=[sedge], writes=[eh_in])
                k.op("v", lambda e: e.tensor_scalar(sv[:, :, 0], ev[:, :, 1],
                                                    flg.t[:, 0:1], None, ALU.mult),
                     reads=[edge, flg], writes=[sedge])
                k.op("v", lambda e: e.memset(sv[:, :, 1], 0.0), writes=[sedge])
                k.op("q", lambda e: e.dma_start(
                    out=_apfull(eh_in.ap(), NCT * 128 * 2, [[2, 128], [256, NCT], [1, 2]]),
                in_=sedge.ap()),
                    reads=[sedge], writes=[eh_in])
                k.op("cc", lambda e: e.collective_compute(
                    "ReduceScatter", ALU.add, replica_groups=RG,
                    ins=[eh_in.ap()], outs=[eh_out.ap()]),
                    reads=[eh_in], writes=[eh_out])
                k.op("gd", lambda e: e.dma_start(
                    out=hrecv.ap(),
                in_=_apfull(eh_out.ap(), 0, [[2, 128], [256, NCT], [1, 2]])),
                    reads=[eh_out], writes=[hrecv])
                k.join()

        with ExitStack() as ces:
            ph = [k.ps(ces, f"s{sub}_ph{i}", [128, 512]) for i in range(2)]
            ptb = [k.ps(ces, f"s{sub}_ptbc{i}", [128, 512], BF16) for i in range(2)]
            cvb = [k.sb(ces, f"s{sub}_cv{i}", [128, TOK], BF16) for i in range(2)]
            mxs = k.sb(ces, f"s{sub}_mxs", [128, 1], F32)
            k.op("v", lambda e: e.memset(mxcol.ap(), 0.0), writes=[mxcol])
            nchunks = [(0, 512), (512, 512), (1024, 512), (1536, 512), (2048, 4)]
            cw, cb = conv_cols["w"], conv_cols["b"]
            for ct in range(NCT):
                wb = w1b[ct % 2]
                hbuf = hb[ct]
                cbuf = cvb[ct % 2]
                k.op("q", lambda e, ct=ct, wb=wb: e.dma_start(
                    out=_strided(wb.ap(), 0, [[128, 8], [1, 128]]),
                    in_=_apfull(wq1c.ap(), ct * 128,
                                [[DH, 128], [128 * DH, 8], [1, 128]])),
                    reads=[wq1c], writes=[wb])
                for ci, (n0, nn) in enumerate(nchunks):
                    pe = ph[ci % 2]
                    for ki in range(8):
                        k.op("t", lambda e, ki=ki, n0=n0, nn=nn, wb=wb, pe=pe: e.matmul(
                            pe.t[:, 0:nn], wb.t[:, ki * 128:(ki + 1) * 128],
                            xqT.t[:, TH * ki + n0: TH * ki + n0 + nn],
                            start=(ki == 0), stop=(ki == 7)),
                            reads=[wb, xqT], writes=[pe], inc=(ki == 7))
                    k.op("v", lambda e, n0=n0, nn=nn, pe=pe, hbuf=hbuf: e.tensor_tensor(
                        hbuf.t[:, n0:n0 + nn], pe.t[:, 0:nn],
                        st1rep.t[:, n0:n0 + nn], ALU.mult),
                        reads=[pe, st1rep], writes=[hbuf])
                if not first:
                    k.op("v", lambda e, ct=ct, hbuf=hbuf: e.tensor_copy(
                        hbuf.t[:, 0:1], hrecv.t[:, 2 * ct:2 * ct + 1]),
                        reads=[hrecv], writes=[hbuf])
                    k.op("v", lambda e, ct=ct, hbuf=hbuf: e.tensor_copy(
                        hbuf.t[:, 2049:2050], hrecv.t[:, 2 * ct + 1:2 * ct + 2]),
                        reads=[hrecv], writes=[hbuf])
                k.op("v", lambda e, ct=ct, hbuf=hbuf, cbuf=cbuf: e.tensor_scalar(
                    cbuf.ap(), hbuf.t[:, 1:2049], cw.t[:, 3 * ct + 1:3 * ct + 2],
                    cb.t[:, ct:ct + 1], ALU.mult, ALU.add),
                    reads=[hbuf, cw, cb], writes=[cbuf])
                k.op("v", lambda e, ct=ct, hbuf=hbuf, cbuf=cbuf: e.scalar_tensor_tensor(
                    cbuf.ap(), hbuf.t[:, 0:2048], cw.t[:, 3 * ct:3 * ct + 1], cbuf.ap(),
                    ALU.mult, ALU.add),
                    reads=[hbuf, cw, cbuf], writes=[cbuf])
                k.op("v", lambda e, ct=ct, hbuf=hbuf, cbuf=cbuf: e.scalar_tensor_tensor(
                    cbuf.ap(), hbuf.t[:, 2:2050], cw.t[:, 3 * ct + 2:3 * ct + 3], cbuf.ap(),
                    ALU.mult, ALU.add),
                    reads=[hbuf, cw, cbuf], writes=[cbuf])
                k.op("s", lambda e, hbuf=hbuf, cbuf=cbuf: e.activation(
                    hbuf.t[:, 1:2049], cbuf.ap(), AF.Silu),
                    reads=[cbuf], writes=[hbuf])
                if ct % 4 == 3:
                    for tb in range(NT):
                        pb = ptb[tb % 2]
                        for q in range(4):
                            k.op("t", lambda e, q=q, tb=tb, pb=pb: e.transpose(
                                pb.t[:, q * 128:(q + 1) * 128],
                                hb[ct - 3 + q].t[:, 1 + tb * 128:1 + (tb + 1) * 128],
                                g["eye"].ap()),
                                reads=[hb[ct - 3 + q], g["eye"]], writes=[pb])
                        k.op("v", lambda e, tb=tb, pb=pb: e.reduce_max(
                            mxs.ap(), pb.ap(), axis=AX.X,
                            apply_absolute_value=True),
                            reads=[pb], writes=[mxs])
                        k.op("v", lambda e, tb=tb: e.tensor_max(
                            mxcol.t[:, tb:tb + 1], mxcol.t[:, tb:tb + 1],
                            mxs.ap()),
                            reads=[mxs, mxcol], writes=[mxcol])
            k.join()

        with ExitStack() as des:
            prow = k.ps(des, f"s{sub}_prow2", [128, 128])
            k.op("v", lambda e: e.tensor_scalar(st2col.ap(), mxcol.ap(), 1e-5, None,
                                                ALU.max),
                 reads=[mxcol], writes=[st2col])
            k.op("v", lambda e: e.scalar_tensor_tensor(
                st2i05.ap(), st2col.t[:, 0:NT], 0.5 / 127.0,
                _strided(g["swinv"].ap(), 1, [[0, NT]]), ALU.mult, ALU.mult),
                reads=[st2col, g["swinv"]], writes=[st2i05])
            k.op("v", lambda e: e.reciprocal(st2col.ap(), st2col.ap()),
                 reads=[st2col], writes=[st2col])
            k.op("v", lambda e: e.tensor_scalar(st2col.ap(), st2col.ap(), 127.0, None,
                                                ALU.mult),
                 reads=[st2col], writes=[st2col])
            pst2 = k.ps(des, f"s{sub}_pst2", [32, 128])
            k.op("t", lambda e: e.transpose(pst2.ap(), st2col.ap(), g["eyef"].ap()),
                 reads=[st2col, g["eyef"]], writes=[pst2])
            k.op("v", lambda e: e.tensor_copy(strow.ap(), pst2.ap()),
                 reads=[pst2], writes=[strow])
            k.op("gd", lambda e: e.dma_start(
                out=_strided(strowf.ap(), 0, [[128, 18], [1, 128]]),
                in_=strow.t[0:18, :]),
                reads=[strow], writes=[strowf])
            for tt in range(NT):
                k.op("t", lambda e, tt=tt: e.matmul(prow.ap(), g["ones_row_b16"].t[0:1, 0:128],
                                                    strowf.t[:, tt * 128:(tt + 1) * 128],
                                                    start=True, stop=True),
                     reads=[g["ones_row_b16"], strowf], writes=[prow])
                k.op("v", lambda e, tt=tt: e.tensor_copy(
                    st2rep.t[:, 1 + tt * 128:1 + (tt + 1) * 128], prow.ap()),
                    reads=[prow], writes=[st2rep])
            k.join()

        with ExitStack() as ees:
            pm = [k.ps(ees, f"s{sub}_pm{i}", [128, 512]) for i in range(8)]
            w2b = [k.sb(ees, f"s{sub}_w2b{i}", [128, C], BF16) for i in range(4)]
            yt = [k.sb(ees, f"s{sub}_yt{i}", [128, C], F32) for i in range(2)]
            xrt = [k.sb(ees, f"s{sub}_xrt0", [128, C], F32)] * 2
            q2y = [k.sb(ees, f"s{sub}_q2y{i}", [128, 512], F32) for i in range(2)]
            q2z = [k.sb(ees, f"s{sub}_q2z{i}", [128, 512], I16) for i in range(2)]
            for tg in range(4):
                c0 = 1 + 512 * tg
                for ct in range(NCT):
                    hbuf = hb[ct]
                    yb, zb = q2y[ct % 2], q2z[ct % 2]
                    k.op("v", lambda e, hbuf=hbuf, yb=yb, c0=c0: e.tensor_tensor(
                        yb.ap(), hbuf.t[:, c0:c0 + 512], st2rep.t[:, c0:c0 + 512],
                        ALU.mult),
                        reads=[hbuf, st2rep], writes=[yb])
                    k.op("s", lambda e, yb=yb, zb=zb: e.activation(
                        zb.ap(), yb.ap(), AF.Copy, bias=16384.0),
                        reads=[yb], writes=[zb])
                    k.op("v", lambda e, yb=yb, zb=zb: e.scalar_tensor_tensor(
                        yb.ap(), zb.ap(), 0.5 - 16384.0, yb.ap(), ALU.add, ALU.is_le),
                        reads=[zb, yb], writes=[yb])
                    k.op("v", lambda e, yb=yb, zb=zb, hbuf=hbuf, c0=c0:
                         e.scalar_tensor_tensor(
                             hbuf.t[:, c0:c0 + 512], yb.ap(), -16384.0, zb.ap(),
                             ALU.add, ALU.add),
                         reads=[yb, zb], writes=[hbuf])
                    wb = w2b[ct % 4]
                    k.op("q", lambda e, ct=ct, wb=wb: e.dma_start(
                        out=wb.ap(), in_=wq2c.t[ct]),
                        reads=[wq2c], writes=[wb])
                    for ti in range(4):
                        tt = tg * 4 + ti
                        for dch in range(2):
                            pmx = pm[ti * 2 + dch]
                            last_of_ct = (ti == 3 and dch == 1)
                            stop = (ct == NCT - 1)
                            k.op("t", lambda e, ct=ct, tt=tt, dch=dch, wb=wb,
                                 pmx=pmx: e.matmul(
                                     pmx.ap(),
                                     hb[ct].t[:, 1 + tt * 128:1 + (tt + 1) * 128],
                                     wb.t[:, dch * 512:(dch + 1) * 512],
                                     start=(ct == 0), stop=stop),
                                 reads=[hb[ct], wb],
                                 writes=[pmx] if stop else [],
                                 inc=(stop or last_of_ct))
                for ti in range(4):
                    tt = tg * 4 + ti
                    ybuf, xrb = yt[ti % 2], xrt[ti % 2]
                    src = xsrc
                    k.op("q", lambda e, tt=tt, xrb=xrb, src=src: e.dma_start(
                        out=xrb.ap(), in_=src.t[tt * 128:(tt + 1) * 128, :]),
                        reads=[src], writes=[xrb])
                    for dch in range(2):
                        pmx = pm[ti * 2 + dch]
                        k.op("s", lambda e, tt=tt, dch=dch, pmx=pmx, ybuf=ybuf:
                             e.activation(ybuf.t[:, dch * 512:(dch + 1) * 512],
                                          pmx.ap(), AF.Copy,
                                          scale=st2i05.t[:, tt:tt + 1]),
                             reads=[pmx, st2i05], writes=[ybuf])
                    k.op("v", lambda e, ybuf=ybuf, xrb=xrb: e.tensor_add(
                        ybuf.ap(), ybuf.ap(), xrb.ap()),
                        reads=[ybuf, xrb], writes=[ybuf])
                    k.op("o", lambda e, tt=tt, ybuf=ybuf: e.dma_start(
                        out=xdst.t[tt * 128:(tt + 1) * 128, :], in_=ybuf.ap()),
                        reads=[ybuf], writes=[xdst])
            k.join()


def _ln_transpose(k, es, g, sub, xsrc, xlnT, sink=None, dump=None,
                  stage_buf=None):
    NDEP = 4
    with ExitStack() as aes:
        if xlnT is not None:
            stage = None
        elif stage_buf is not None:
            stage = [(stage_buf, 0)] + [(Buf(stage_buf.t), 1024 * j)
                                        for j in range(1, NDEP)]
        else:
            sb0 = k.sb(aes, f"s{sub}_lstg0", [128, NDEP * 8 * 128], BF16)
            stage = [(sb0, 0)] + [(Buf(sb0.t), 1024 * j) for j in range(1, NDEP)]
        ptb = [k.ps(aes, f"s{sub}_lptb{i}", [128, 128], BF16) for i in range(2)]
        xt = [k.sb(aes, f"s{sub}_lxt{i}", [128, C], F32) for i in range(NDEP)]
        xlnb = [k.sb(aes, f"s{sub}_lxln{i}", [128, C], BF16) for i in range(NDEP)]
        sdump = dump if dump is not None else k.sb(aes, f"s{sub}_lsd", [128, C], BF16)
        colsb = [k.sb(aes, f"s{sub}_lcols{i}", [128, 6], F32) for i in range(NDEP)]
        for tt in range(NT):
            i = tt % NDEP
            xb, cols, xl = xt[i], colsb[i], xlnb[i]
            k.op("q", lambda e, tt=tt, xb=xb: e.dma_start(
                out=xb.ap(), in_=xsrc.t[tt * 128:(tt + 1) * 128, :]),
                reads=[xsrc], writes=[xb])
            k.op("v", lambda e, xb=xb, cols=cols: e.reduce_sum(
                cols.t[:, 0:1], xb.ap(), axis=AX.X), reads=[xb], writes=[cols])
            k.op("s", lambda e, xb=xb, cols=cols: e.activation(
                sdump.t[:, 0:C], xb.ap(), AF.Square, accum_out=cols.t[:, 1:2]),
                reads=[xb], writes=[sdump, cols])
            k.op("v", lambda e, cols=cols: e.tensor_scalar(
                cols.t[:, 2:3], cols.t[:, 0:1], 1.0 / C, None, ALU.mult),
                reads=[cols], writes=[cols])
            k.op("v", lambda e, cols=cols: e.tensor_mul(
                cols.t[:, 3:4], cols.t[:, 2:3], cols.t[:, 2:3]),
                reads=[cols], writes=[cols])
            k.op("v", lambda e, cols=cols: e.scalar_tensor_tensor(
                cols.t[:, 3:4], cols.t[:, 1:2], 1.0 / C, cols.t[:, 3:4],
                ALU.mult, ALU.subtract), reads=[cols], writes=[cols])
            k.op("s", lambda e, cols=cols: e.activation(
                cols.t[:, 4:5], cols.t[:, 3:4], AF.Sqrt, bias=g["eps"].ap()),
                reads=[cols, g["eps"]], writes=[cols])
            k.op("v", lambda e, cols=cols: e.reciprocal(
                cols.t[:, 4:5], cols.t[:, 4:5]), reads=[cols], writes=[cols])
            k.op("v", lambda e, cols=cols: e.scalar_tensor_tensor(
                cols.t[:, 5:6], cols.t[:, 2:3], -1.0, cols.t[:, 4:5],
                ALU.mult, ALU.mult), reads=[cols], writes=[cols])
            k.op("s", lambda e, xb=xb, cols=cols, xl=xl: e.activation(
                xl.ap(), xb.ap(), AF.Identity, bias=cols.t[:, 5:6],
                scale=cols.t[:, 4:5]), reads=[xb, cols], writes=[xl])
            stg, soff = stage[tt % NDEP] if stage is not None else (None, 0)
            for j in range(8):
                pb = ptb[j % 2]
                k.op("t", lambda e, j=j, xl=xl, pb=pb: e.transpose(
                    pb.ap(), xl.t[:, j * 128:(j + 1) * 128], g["eye"].ap()),
                    reads=[xl, g["eye"]], writes=[pb])
                if xlnT is not None:
                    k.op("v", lambda e, j=j, tt=tt, pb=pb: e.tensor_copy(
                        xlnT.t[:, TOK * j + tt * 128: TOK * j + (tt + 1) * 128],
                        pb.ap()), reads=[pb], writes=[xlnT])
                else:
                    k.op("v", lambda e, j=j, pb=pb, stg=stg, soff=soff: e.tensor_copy(
                        stg.t[:, soff + 128 * j:soff + 128 * (j + 1)], pb.ap()),
                        reads=[pb], writes=[stg])
            if stage is not None:
                sink(tt, stg, soff)
        k.join()


def _moe(k, g, sub, xsrc, xdst, routerw, expertw):
    with ExitStack() as es:
        xlnT = k.sb(es, f"s{sub}_xlnT", [128, 8 * TOK], BF16)
        _ln_transpose(k, es, g, sub, xsrc, xlnT)
        rw = k.sb(es, f"s{sub}_rw", [128, 9 * 8], BF16)
        yacc = [k.sb(es, f"s{sub}_y{tt}", [128, C], F32) for tt in range(NT)]
        cgate = k.sb(es, f"s{sub}_cg", [128, NT * 8], F32)
        ex = k.sb(es, f"s{sub}_ex", [128, 16], F32)
        gcol = k.sb(es, f"s{sub}_gcol", [128, 2], F32)
        xrt = [k.sb(es, f"s{sub}_xr{i}", [128, C], F32) for i in range(2)]

        k.op("gd", lambda e: e.dma_start(
            out=_strided(rw.ap(), 0, [[8, 8], [1, 8]]),
            in_=_apfull(routerw.ap(), 0, [[8, 128], [1024, 8], [1, 8]])),
            reads=[routerw], writes=[rw])
        k.op("gd", lambda e: e.dma_start(
            out=rw.t[0:1, 64:72], in_=routerw.t[1024:1025, :]),
            reads=[routerw], writes=[rw])

        with ExitStack() as bes:
            plg = k.ps(bes, f"s{sub}_plg", [128, 8])
            for tt in range(NT):
                for ki in range(8):
                    k.op("t", lambda e, tt=tt, ki=ki: e.matmul(
                        plg.ap(), xlnT.t[:, TOK * ki + tt * 128: TOK * ki + (tt + 1) * 128],
                        rw.t[:, 8 * ki:8 * ki + 8], start=(ki == 0), stop=False),
                        reads=[xlnT, rw], writes=[plg], inc=False)
                k.op("t", lambda e, tt=tt: e.matmul(
                    plg.ap(), g["ones_row_b16"].t[0:1, 0:128], rw.t[0:1, 64:72],
                    start=False, stop=True),
                    reads=[g["ones_row_b16"], rw], writes=[plg])
                k.op("v", lambda e: e.reduce_max(gcol.t[:, 0:1], plg.ap(), axis=AX.X),
                     reads=[plg], writes=[gcol])
                k.op("v", lambda e: e.tensor_scalar(gcol.t[:, 0:1], gcol.t[:, 0:1],
                                                    -1.0, None, ALU.mult),
                     reads=[gcol], writes=[gcol])
                k.op("s", lambda e: e.activation(ex.t[:, 0:8], plg.ap(), AF.Exp,
                                                 bias=gcol.t[:, 0:1]),
                     reads=[plg, gcol], writes=[ex])
                k.op("v", lambda e: e.reduce_sum(gcol.t[:, 1:2], ex.t[:, 0:8], axis=AX.X),
                     reads=[ex], writes=[gcol])
                k.op("v", lambda e: e.reciprocal(gcol.t[:, 1:2], gcol.t[:, 1:2]),
                     reads=[gcol], writes=[gcol])
                k.op("v", lambda e: e.tensor_scalar(ex.t[:, 8:16], ex.t[:, 0:8],
                                                    1.0, None, ALU.is_equal),
                     reads=[ex], writes=[ex])
                k.op("v", lambda e, tt=tt: e.scalar_tensor_tensor(
                    cgate.t[:, 8 * tt:8 * tt + 8], ex.t[:, 8:16], 0.5,
                    _strided(gcol.ap(), 1, [[0, 8]]), ALU.mult, ALU.mult),
                    reads=[ex, gcol], writes=[cgate])
            k.join()

        _moe_experts(k, es, g, sub, xlnT, expertw, cgate, yacc, xsrc, xdst, xrt)


def _moe_experts(k, es, g, sub, xlnT, expertw, cgate, yacc, xsrc, xdst, xrt):
    with ExitStack() as ces:
        pm = [k.ps(ces, f"s{sub}_epm{i}", [128, 512]) for i in range(8)]
        web = [k.sb(ces, f"s{sub}_ewe{i}", [128, C], BF16) for i in range(9)]
        webr = k.sb(ces, f"s{sub}_ewebr", [1, C], BF16)
        for e_ in range(NE):
            k.op("gd", lambda e, e_=e_: e.dma_start(
                out=webr.ap(), in_=expertw.t[e_, 1024:1025, :]),
                reads=[expertw], writes=[webr])
            for ki in range(8):
                wb = web[(e_ + ki) % 9]
                k.op("gd", lambda e, e_=e_, ki=ki, wb=wb: e.dma_start(
                    out=wb.ap(), in_=expertw.t[e_, ki * 128:(ki + 1) * 128, :]),
                    reads=[expertw], writes=[wb])
            for tg in range(4):
                for ki in range(8):
                    wb = web[(e_ + ki) % 9]
                    for ti in range(4):
                        tt = tg * 4 + ti
                        for dch in range(2):
                            pmx = pm[2 * ti + dch]
                            last_of_ki = (ti == 3 and dch == 1)
                            k.op("t", lambda e, tt=tt, ki=ki, dch=dch, wb=wb,
                                 pmx=pmx: e.matmul(
                                     pmx.ap(),
                                     xlnT.t[:, TOK * ki + tt * 128: TOK * ki + (tt + 1) * 128],
                                     wb.t[:, dch * 512:(dch + 1) * 512],
                                     start=(ki == 0), stop=False),
                                 reads=[xlnT, wb],
                                 writes=[pmx] if ki == 0 else [],
                                 inc=last_of_ki)
                for ti in range(4):
                    tt = tg * 4 + ti
                    for dch in range(2):
                        pmx = pm[2 * ti + dch]
                        k.op("t", lambda e, tt=tt, dch=dch, pmx=pmx: e.matmul(
                            pmx.ap(), g["ones_row_b16"].t[0:1, 0:128],
                            webr.t[0:1, dch * 512:(dch + 1) * 512],
                            start=False, stop=True),
                            reads=[g["ones_row_b16"], webr], writes=[pmx])
                        ya = yacc[tt]
                        if e_ == 0:
                            k.op("v", lambda e, tt=tt, e_=e_, dch=dch, pmx=pmx,
                                 ya=ya: e.tensor_scalar(
                                     ya.t[:, dch * 512:(dch + 1) * 512], pmx.ap(),
                                     cgate.t[:, 8 * tt + e_:8 * tt + e_ + 1],
                                     None, ALU.mult),
                                 reads=[pmx, cgate], writes=[ya])
                        else:
                            k.op("v", lambda e, tt=tt, e_=e_, dch=dch, pmx=pmx,
                                 ya=ya: e.scalar_tensor_tensor(
                                     ya.t[:, dch * 512:(dch + 1) * 512], pmx.ap(),
                                     cgate.t[:, 8 * tt + e_:8 * tt + e_ + 1],
                                     ya.t[:, dch * 512:(dch + 1) * 512],
                                     ALU.mult, ALU.add),
                                 reads=[pmx, cgate, ya], writes=[ya])
        for tt in range(NT):
            xrb = xrt[tt % 2]
            k.op("q", lambda e, tt=tt, xrb=xrb: e.dma_start(
                out=xrb.ap(), in_=xsrc.t[tt * 128:(tt + 1) * 128, :]),
                reads=[xsrc], writes=[xrb])
            k.op("v", lambda e, tt=tt, xrb=xrb: e.tensor_add(
                yacc[tt].ap(), yacc[tt].ap(), xrb.ap()),
                reads=[yacc[tt], xrb], writes=[yacc[tt]])
            k.op("o", lambda e, tt=tt: e.dma_start(
                out=xdst.t[tt * 128:(tt + 1) * 128, :], in_=yacc[tt].ap()),
                reads=[yacc[tt]], writes=[xdst])
        k.join()


def _attention(k, g, sub, layer, xsrc, xdst, qkvw, projw, slopes8,
               ag_in, ag_out, rs_in, rs_out):
    NQB = 8
    with ExitStack() as es:
        attnP = k.sb(es, f"s{sub}_attnP", [128, (NH // 2) * NQB * 512], BF16)
        s123 = es.enter_context(ExitStack())
        xlnTF = k.sb(s123, f"s{sub}_xlnTF", [128, 8 * N], BF16)
        qbuf = [k.sb(s123, f"s{sub}_qb{i}", [70, N], BF16) for i in range(2)]
        kbuf = [k.sb(s123, f"s{sub}_kb{i}", [70, N], BF16) for i in range(2)]
        if layer == 0:
            cost = k.sb(s123, f"s{sub}_cost", [128, 2048], BF16)
            sint = k.sb(s123, f"s{sub}_sint", [128, 2048], BF16)
        else:
            cost = sint = None
        vbuf = [k.sb(s123, f"s{sub}_vb{i}", [128, 32 * 130], BF16) for i in range(2)]
        wqkv = k.sb(s123, f"s{sub}_wqkv", [128, 8 * 384], BF16)
        biasq = k.sb(s123, f"s{sub}_biasq", [128, 64], F32)
        ccol = k.sb(s123, f"s{sub}_ccol", [128, 8], F32)
        bcol2 = k.sb(s123, f"s{sub}_bcol2", [128, 64], F32)
        kposc = k.sb(s123, f"s{sub}_kposc", [128, 32], F32)
        dtile = [k.sb(s123, f"s{sub}_dt{j}", [128, 512], F16) for j in range(4)]
        coefr = k.sb(s123, f"s{sub}_coefr", [48, 512], BF16)
        rrow = k.sb(s123, f"s{sub}_rrow", [1, 512], F32)
        nrm = {
            "cols": [k.sb(s123, f"s{sub}_ncol{i}", [128, 64], F32) for i in range(2)],
            "dump": k.sb(s123, f"s{sub}_ndump", [128, 128], BF16),
            "rowq": k.sb(s123, f"s{sub}_nrq", [1, 16], F32),
            "rowk": k.sb(s123, f"s{sub}_nrk", [1, 16], F32),
            "work": k.sb(s123, f"s{sub}_nwork", [1, 16], F32),
        }
        mC = k.sb(s123, f"s{sub}_mC", [128, 16], F32)

        with ExitStack() as aes:
            def sink(tt, stg, soff):
                k.op("q", lambda e, tt=tt, stg=stg, soff=soff: e.dma_start(
                    out=_apfull(ag_in.ap(), 128 * tt,
                                [[TOK, 128], [128 * TOK, 8], [1, 128]]),
                    in_=_strided(stg.ap(), soff, [[128, 8], [1, 128]])),
                    reads=[stg], writes=[ag_in])
            _ln_transpose(k, aes, g, sub, xsrc, None, sink=sink,
                          dump=vbuf[1], stage_buf=vbuf[0])
            k.op("cc", lambda e: e.collective_compute(
                "AllGather", ALU.bypass, replica_groups=RG,
                ins=[ag_in.ap()], outs=[ag_out.ap()]),
                reads=[ag_in], writes=[ag_out])
            for r in range(2):
                k.op("q", lambda e, r=r: e.dma_start(
                    out=_strided(xlnTF.ap(), r * TOK, [[N, 8], [1, TOK]]),
                    in_=_apfull(ag_out.ap(), r * 8 * 128 * TOK,
                                [[TOK, 128], [128 * TOK, 8], [1, TOK]])),
                    reads=[ag_out], writes=[xlnTF])
            k.join()

        k.op("g", lambda e: e.iota(kposc.ap(), pattern=[[128, 32]], base=0,
                                   channel_multiplier=1,
                                   allow_small_or_imprecise_dtypes=True),
             writes=[kposc])
        if layer == 0:
            k.op("g", lambda e: e.iota(
                attnP.t[0:1, 0:512], pattern=[[1, 32], [0, 16]],
                base=0, channel_multiplier=0,
                allow_small_or_imprecise_dtypes=True), writes=[attnP])
            k.op("g", lambda e: e.iota(
                attnP.t[0:1, 8192:8704], pattern=[[0, 32], [1, 16]],
                base=0, channel_multiplier=0,
                allow_small_or_imprecise_dtypes=True), writes=[attnP])
            for w in (512, 1024, 2048):
                k.op("v", lambda e, w=w: e.tensor_copy(
                    attnP.t[0:1, w:2 * w], attnP.t[0:1, 0:w]),
                    reads=[attnP], writes=[attnP])
                k.op("v", lambda e, w=w: e.tensor_copy(
                    attnP.t[0:1, 8192 + w:8192 + 2 * w],
                    attnP.t[0:1, 8192:8192 + w]),
                    reads=[attnP], writes=[attnP])
            for i in range(2):
                for r in range(6):
                    srco = 0 if r < 3 else 8192
                    k.op("gd", lambda e, i=i, r=r, srco=srco: e.dma_start(
                        out=qbuf[i].t[64 + r:65 + r, 0:4096],
                        in_=attnP.t[0:1, srco:srco + 4096]),
                        reads=[attnP], writes=[qbuf[i]])
            with ExitStack() as bes:
                pbc = k.ps(bes, f"s{sub}_pbc", [128, 8])
                slrow = k.sb(bes, f"s{sub}_slrow", [1, 32], F32)
                k.op("q", lambda e: e.dma_start(out=slrow.t[0:1, 0:8],
                                                in_=slopes8.ap()),
                     reads=[slopes8], writes=[slrow])
                pco = k.ps(bes, f"s{sub}_pco", [48, 512])
                jio = k.sb(bes, f"s{sub}_jio", [128, 512], F32)
                cval = k.sb(bes, f"s{sub}_cval", [1, 48], BF16)
                k.op("t", lambda e: e.matmul(pbc.ap(), g["ones_row_f"].ap(),
                                             slrow.t[0:1, 0:8], start=True, stop=True),
                     reads=[g["ones_row_f"], slrow], writes=[pbc])
                k.op("v", lambda e: e.tensor_scalar(ccol.ap(), pbc.ap(), 8.0, None,
                                                    ALU.mult), reads=[pbc], writes=[ccol])
                for h in range(NH):
                    for qb in range(NQB):
                        k.op("v", lambda e, h=h, qb=qb: e.tensor_scalar(
                            bcol2.t[:, 8 * h + qb:8 * h + qb + 1],
                            ccol.t[:, h:h + 1], float(-512 * qb), -26.0,
                            ALU.mult, ALU.add), reads=[ccol], writes=[bcol2])
                k.op("v", lambda e: e.tensor_scalar(
                    _strided(slrow.ap(), 8, [[1, 8]]), slrow.t[0:1, 0:8],
                    -128.0, None, ALU.mult), reads=[slrow], writes=[slrow])
                k.op("v", lambda e: e.tensor_scalar(
                    _strided(slrow.ap(), 16, [[1, 8]]), slrow.t[0:1, 0:8],
                    -8.0, None, ALU.mult), reads=[slrow], writes=[slrow])
                for base, cv0 in ((8, 0), (16, 3)):
                    for lvl in range(3):
                        k.op("v", lambda e, base=base, cv0=cv0, lvl=lvl: e.tensor_copy(
                            _strided(cval.ap(), cv0 + lvl, [[6, 8]]),
                            _strided(slrow.ap(), base, [[1, 8]])),
                            reads=[slrow], writes=[cval])
                        if lvl < 2:
                            k.op("v", lambda e, base=base, cv0=cv0, lvl=lvl:
                                 e.tensor_sub(
                                     _strided(slrow.ap(), base, [[1, 8]]),
                                     _strided(slrow.ap(), base, [[1, 8]]),
                                     _strided(cval.ap(), cv0 + lvl, [[6, 8]])),
                                 reads=[slrow, cval], writes=[slrow])
                k.op("t", lambda e: e.matmul(pco.ap(), cval.ap(),
                                             g["ones_row_b16"].ap(),
                                             start=True, stop=True),
                     reads=[cval, g["ones_row_b16"]], writes=[pco])
                k.op("v", lambda e: e.tensor_copy(coefr.ap(), pco.ap()),
                     reads=[pco], writes=[coefr])
                k.op("g", lambda e: e.iota(jio.ap(), pattern=[[1, 512]], base=0,
                                           channel_multiplier=0,
                                           allow_small_or_imprecise_dtypes=True),
                     writes=[jio])
                for jj in range(4):
                    k.op("g", lambda e, jj=jj: e.iota(
                        dtile[jj].ap(), pattern=[[0, 512]], base=128 * jj,
                        channel_multiplier=1,
                        allow_small_or_imprecise_dtypes=True), writes=[dtile[jj]])
                    k.op("v", lambda e, jj=jj: e.tensor_tensor(
                        dtile[jj].ap(), dtile[jj].ap(), jio.ap(), ALU.min),
                        reads=[dtile[jj], jio], writes=[dtile[jj]])
                k.join()
            _build_rope_tables(k, es, g, sub, cost, sint)
        else:
            with ExitStack() as bes:
                jio = k.sb(bes, f"s{sub}_jio", [128, 512], F32)
                k.op("g", lambda e: e.iota(jio.ap(), pattern=[[1, 512]], base=0,
                                           channel_multiplier=0,
                                           allow_small_or_imprecise_dtypes=True),
                     writes=[jio])
                for jj in range(4):
                    k.op("g", lambda e, jj=jj: e.iota(
                        dtile[jj].ap(), pattern=[[0, 512]], base=128 * jj,
                        channel_multiplier=1,
                        allow_small_or_imprecise_dtypes=True), writes=[dtile[jj]])
                    k.op("v", lambda e, jj=jj: e.tensor_tensor(
                        dtile[jj].ap(), dtile[jj].ap(), jio.ap(), ALU.is_le),
                        reads=[dtile[jj], jio], writes=[dtile[jj]])
                    k.op("v", lambda e, jj=jj: e.tensor_scalar(
                        dtile[jj].ap(), dtile[jj].ap(), -1.0, 1e30,
                        ALU.add, ALU.mult), reads=[dtile[jj]], writes=[dtile[jj]])
                k.join()
        k.join()

        with ExitStack() as ces:
            pq = [k.ps(ces, f"s{sub}_pq{i}", [128, 512]) for i in range(2)]
            ptr_t = k.ps(ces, f"s{sub}_ptr", [128, 256], BF16)
            ptr = [ptr_t, Buf(ptr_t.t)]
            pe2 = [k.ps(ces, f"s{sub}_pe{i}", [128, 512]) for i in range(2)]
            pe = [pq[0], pq[1], pe2[0], pe2[1]]
            pv = [k.ps(ces, f"s{sub}_pv{i}", [65, 512]) for i in range(2)]
            prep = k.ps(ces, f"s{sub}_prep", [65, 512])
            ropes = [k.sb(ces, f"s{sub}_ropes_{j}", [128, 128], F32)
                     for j in range(2)]
            ropes2 = [k.sb(ces, f"s{sub}_ropes2_{j}", [128, 128], F32)
                      for j in range(2)]
            ropesb = [k.sb(ces, f"s{sub}_ropesb_{j}", [128, 128], BF16)
                      for j in range(2)]
            preps = k.sb(ces, f"s{sub}_preps", [65, 512], F32)
            nrm["col2"] = k.sb(ces, f"s{sub}_ncol2", [128, 2], F32)
            nrm["pn"] = pe2[0]
            nrm["pn2"] = pe2[1]
            nrm["pmc"] = pq[0]
            wT = [k.sb(ces, f"s{sub}_wT{i}", [128, 512], BF16) for i in range(4)]

            for vb in vbuf:
                k.op("v", lambda e, vb=vb: e.memset(
                    _strided(vb.ap(), 64, [[65, 64]]), 1.0), writes=[vb])

            for hp in range(4):
                wq_b = wqkv
                vb_b = vbuf[hp % 2]
                for part in range(3):
                    k.op("gd", lambda e, hp=hp, part=part, wq_b=wq_b: e.dma_start(
                        out=_strided(wq_b.ap(), 128 * part, [[384, 8], [1, 128]]),
                        in_=_apfull(qkvw.ap(), 512 * part + 128 * hp,
                                    [[1536, 128], [128 * 1536, 8], [1, 128]])),
                        reads=[qkvw], writes=[wq_b])
                for ki in range(8):
                    k.op("v", lambda e, ki=ki, wq_b=wq_b: e.tensor_scalar(
                        wq_b.t[:, 384 * ki:384 * ki + 128],
                        wq_b.t[:, 384 * ki:384 * ki + 128], 0.125, None, ALU.mult),
                        reads=[wq_b], writes=[wq_b])
                if layer == 0:
                    for i, h in enumerate((2 * hp, 2 * hp + 1)):
                        kb = kbuf[i]
                        for nc_ in range(8):
                            k.op("q", lambda e, h=h, kb=kb, nc_=nc_: e.dma_start(
                                out=kb.t[64:70, 512 * nc_:512 * (nc_ + 1)],
                                in_=coefr.t[6 * h:6 * h + 6, :]),
                                reads=[coefr], writes=[kb])
                _qkv_rope(k, g, sub, hp, wq_b, xlnTF, qbuf, kbuf,
                          vb_b, pq, ptr, ropes, ropes2, ropesb,
                          cost, sint, nrm, use_rope=(layer == 0))
                _shift_consts(k, g, sub, layer, hp, nrm, mC)
                for i in range(2):
                    h = 2 * hp + i
                    if layer == 0:
                        for qb in range(8):
                            k.op("v", lambda e, h=h, i=i, qb=qb:
                                 e.scalar_tensor_tensor(
                                     bcol2.t[:, 8 * h + qb:8 * h + qb + 1],
                                     ccol.t[:, h:h + 1], float(-512 * qb),
                                     mC.t[:, 8 * i + qb:8 * i + qb + 1],
                                     ALU.mult, ALU.add),
                                 reads=[ccol, mC], writes=[bcol2])

                    def q_ap(KK, qb, i=i):
                        return qbuf[i].t[0:KK, 512 * qb:512 * (qb + 1)]

                    def k_ap(KK, kc, i=i):
                        return kbuf[i].t[0:KK, 128 * kc:128 * (kc + 1)]
                    qsrc, ksrc = qbuf[i], kbuf[i]
                    _attn_head(k, g, sub, layer, h, i, q_ap, k_ap, qsrc, ksrc,
                               vb_b, attnP, pe, pv, prep, preps, wT, ccol, bcol2,
                               kposc, biasq, dtile, mC, rrow)
            k.join()

        k.join()
        s123.close()
        with ExitStack() as des:
            pp = [k.ps(des, f"s{sub}_pp{i}", [128, 512]) for i in range(4)]
            pst = [k.sb(des, f"s{sub}_pst{i}", [128, C], F32) for i in range(2)]
            xrt2 = [k.sb(des, f"s{sub}_xrt{i}", [128, C], F32) for i in range(2)]
            rst = [k.sb(des, f"s{sub}_rst{i}", [128, C], F32) for i in range(2)]
            pwh = k.sb(des, f"s{sub}_pwh", [128, (NH // 2) * C], BF16)
            pwb = k.sb(des, f"s{sub}_pwb", [1, C], BF16)
            k.op("gd", lambda e: e.dma_start(
                out=pwh.t[0:64, :],
                in_=_apfull(projw.ap(), 0, [[C, 64], [128 * C, NH // 2], [1, C]])),
                reads=[projw], writes=[pwh])
            k.op("gd", lambda e: e.dma_start(
                out=pwh.t[64:128, :],
                in_=_apfull(projw.ap(), 64 * C, [[C, 64], [128 * C, NH // 2], [1, C]])),
                reads=[projw], writes=[pwh])
            k.op("gd", lambda e: e.dma_start(
                out=pwb.ap(), in_=projw.t[512:513, :]), reads=[projw], writes=[pwb])
            rs_in_c = [rs_in] + [Buf(rs_in.t) for _ in range(3)]
            rs_out_c = [rs_out] + [Buf(rs_out.t) for _ in range(3)]
            for cch in range(4):
                for half in range(2):
                    for t4 in range(4):
                        ts = 16 * half + 4 * cch + t4
                        qb, tsl = ts // 4, ts % 4
                        st = pst[ts % 2]
                        for dch in range(2):
                            pmx = pp[2 * (ts % 2) + dch]
                            for h in range(NH // 2):
                                k.op("t", lambda e, h=h, qb=qb, tsl=tsl, dch=dch,
                                     pmx=pmx: e.matmul(
                                         pmx.ap(),
                                         attnP.t[:, 4096 * h + 512 * qb + 128 * tsl:
                                                 4096 * h + 512 * qb + 128 * (tsl + 1)],
                                         pwh.t[:, C * h + 512 * dch:C * h + 512 * (dch + 1)],
                                         start=(h == 0), stop=False),
                                     reads=[attnP, pwh],
                                     writes=[pmx] if h == 0 else [], inc=False)
                            k.op("t", lambda e, dch=dch, pmx=pmx: e.matmul(
                                pmx.ap(), g["ones_row_b16"].t[0:1, 0:128],
                                pwb.t[0:1, dch * 512:(dch + 1) * 512],
                                start=False, stop=True),
                                reads=[g["ones_row_b16"], pwb], writes=[pmx])
                            k.op("v", lambda e, dch=dch, pmx=pmx, st=st: e.tensor_copy(
                                st.t[:, dch * 512:(dch + 1) * 512], pmx.ap()),
                                reads=[pmx], writes=[st])
                        k.op("q", lambda e, cch=cch, half=half, t4=t4, st=st:
                             e.dma_start(
                                 out=rs_in.t[cch, half, t4 * 128:(t4 + 1) * 128, :],
                                 in_=st.ap()),
                             reads=[st], writes=[rs_in_c[cch]])
                k.op("cc", lambda e, cch=cch: e.collective_compute(
                    "ReduceScatter", ALU.add, replica_groups=RG,
                    ins=[rs_in.t[cch]], outs=[rs_out.t[cch]]),
                    reads=[rs_in_c[cch]], writes=[rs_out_c[cch]])
            for cch in range(4):
                for t4 in range(4):
                    tt = 4 * cch + t4
                    rb = rst[tt % 2]
                    xrb = xrt2[tt % 2]
                    k.op("gd", lambda e, cch=cch, t4=t4, rb=rb: e.dma_start(
                        out=rb.ap(), in_=rs_out.t[cch, t4 * 128:(t4 + 1) * 128, :]),
                        reads=[rs_out_c[cch]], writes=[rb])
                    k.op("gd", lambda e, tt=tt, xrb=xrb: e.dma_start(
                        out=xrb.ap(), in_=xsrc.t[tt * 128:(tt + 1) * 128, :]),
                        reads=[xsrc], writes=[xrb])
                    k.op("v", lambda e, rb=rb, xrb=xrb: e.tensor_add(
                        rb.ap(), rb.ap(), xrb.ap()),
                        reads=[rb, xrb], writes=[rb])
                    k.op("o", lambda e, tt=tt, rb=rb: e.dma_start(
                        out=xdst.t[tt * 128:(tt + 1) * 128, :], in_=rb.ap()),
                        reads=[rb], writes=[xdst])
            k.join()


def _shift_consts(k, g, sub, layer, hp, nrm, mC):
    rowq, rowk, work = nrm["rowq"], nrm["rowk"], nrm["work"]
    for i in range(2):
        if layer == 1:
            for qb in range(1, 8):
                k.op("v", lambda e, i=i, qb=qb: e.tensor_max(
                    rowk.t[0:1, 8 * i + qb:8 * i + qb + 1],
                    rowk.t[0:1, 8 * i + qb:8 * i + qb + 1],
                    rowk.t[0:1, 8 * i + qb - 1:8 * i + qb]),
                    reads=[rowk], writes=[rowk])
            k.op("v", lambda e, i=i: e.tensor_mul(
                work.t[0:1, 0:8], rowq.t[0:1, 8 * i:8 * i + 8],
                rowk.t[0:1, 8 * i:8 * i + 8]),
                reads=[rowq, rowk], writes=[work])
        else:
            k.op("v", lambda e, i=i: e.tensor_reduce(
                work.t[0:1, 8:9], rowk.t[0:1, 8 * i:8 * i + 8], AX.X, ALU.max),
                reads=[rowk], writes=[work])
            k.op("v", lambda e, i=i: e.tensor_scalar(
                work.t[0:1, 0:8], rowq.t[0:1, 8 * i:8 * i + 8],
                work.t[0:1, 8:9], None, ALU.mult),
                reads=[rowq, work], writes=[work])
        k.op("s", lambda e: e.activation(work.t[0:1, 0:8], work.t[0:1, 0:8],
                                         AF.Sqrt),
             reads=[work], writes=[work])
        k.op("v", lambda e: e.tensor_scalar(work.t[0:1, 0:8], work.t[0:1, 0:8],
                                            -1.0, 80.0, ALU.mult, ALU.add),
             reads=[work], writes=[work])
        k.op("v", lambda e: e.tensor_scalar(work.t[0:1, 0:8], work.t[0:1, 0:8],
                                            0.0, None, ALU.min),
             reads=[work], writes=[work])
        k.op("t", lambda e, i=i: e.matmul(
            nrm["pmc"].t[:, 0:8], g["ones_row_f"].ap(), work.t[0:1, 0:8],
            start=True, stop=True),
            reads=[g["ones_row_f"], work], writes=[nrm["pmc"]])
        k.op("v", lambda e, i=i: e.tensor_copy(mC.t[:, 8 * i:8 * i + 8],
                                               nrm["pmc"].t[:, 0:8]),
             reads=[nrm["pmc"]], writes=[mC])


def _qkv_rope(k, g, sub, hp, wqkv, xlnTF, qbuf, kbuf, vbuf, pq, ptr,
              ropes, ropes2, ropesb, cost, sint, nrm, use_rope):
    eye = g["eye"]
    ncol = nrm["cols"]
    for tt in range(32):
        pqx = pq[tt % 2]
        for ki in range(8):
            k.op("t", lambda e, tt=tt, ki=ki, pqx=pqx: e.matmul(
                pqx.t[:, 0:384],
                xlnTF.t[:, N * ki + 128 * tt: N * ki + 128 * (tt + 1)],
                wqkv.t[:, 384 * ki:384 * ki + 384],
                start=(ki == 0), stop=(ki == 7)),
                reads=[xlnTF, wqkv], writes=[pqx], inc=(ki == 7))
        for j, (off, dst2, ncb) in enumerate(((0, qbuf, 0), (128, kbuf, 32))):
            rs, rs2, rsb, pt = ropes[j], ropes2[j], ropesb[j], ptr[j]
            if use_rope:
                k.op("v", lambda e, pqx=pqx, tt=tt, off=off, rs=rs: e.tensor_tensor(
                    rs.ap(), pqx.t[:, off:off + 128],
                    _strided(cost.ap(), 64 * tt, [[0, 2], [1, 64]]), ALU.mult),
                    reads=[pqx, cost], writes=[rs])
                k.op("v", lambda e, pqx=pqx, tt=tt, off=off, rs2=rs2: e.tensor_tensor(
                    _strided(rs2.ap(), 0, [[64, 2], [1, 32]]),
                    _strided(pqx.ap(), off + 32, [[64, 2], [1, 32]]),
                    _strided(sint.ap(), 64 * tt, [[0, 2], [1, 32]]), ALU.mult),
                    reads=[pqx, sint], writes=[rs2])
                k.op("v", lambda e, pqx=pqx, tt=tt, off=off, rs2=rs2: e.tensor_tensor(
                    _strided(rs2.ap(), 32, [[64, 2], [1, 32]]),
                    _strided(pqx.ap(), off, [[64, 2], [1, 32]]),
                    _strided(sint.ap(), 64 * tt + 32, [[0, 2], [1, 32]]), ALU.mult),
                    reads=[pqx, sint], writes=[rs2])
                k.op("v", lambda e, rs=rs, rs2=rs2, rsb=rsb: e.tensor_add(
                    rsb.ap(), rs.ap(), rs2.ap()),
                    reads=[rs, rs2], writes=[rsb])
            else:
                k.op("v", lambda e, pqx=pqx, off=off, rsb=rsb: e.tensor_copy(
                    rsb.ap(), pqx.t[:, off:off + 128]),
                    reads=[pqx], writes=[rsb])
            for i in range(2):
                k.op("s", lambda e, i=i, tt=tt, ncb=ncb, rsb=rsb: e.activation(
                    nrm["dump"].t[:, 64 * i:64 * (i + 1)],
                    rsb.t[:, 64 * i:64 * (i + 1)], AF.Square,
                    accum_out=ncol[i].t[:, ncb + tt:ncb + tt + 1]),
                    reads=[rsb], writes=[nrm["dump"], ncol[i]])
            k.op("t", lambda e, j=j, rsb=rsb, pt=pt: e.transpose(
                pt.t[:, 128 * j:128 * (j + 1)], rsb.ap(), eye.ap()),
                reads=[rsb, eye], writes=[pt])
            for i in range(2):
                dst = dst2[i]
                k.op("v", lambda e, tt=tt, dst=dst, i=i, j=j, pt=pt: e.tensor_copy(
                    dst.t[0:64, 128 * tt:128 * (tt + 1)],
                    pt.t[64 * i:64 * (i + 1), 128 * j:128 * (j + 1)]),
                    reads=[pt], writes=[dst])
        for i in range(2):
            k.op("v", lambda e, tt=tt, i=i, pqx=pqx: e.tensor_copy(
                vbuf.t[:, 130 * tt + 65 * i:130 * tt + 65 * i + 64],
                pqx.t[:, 256 + 64 * i:256 + 64 * (i + 1)]),
                reads=[pqx], writes=[vbuf])
    for isq, ncb in ((True, 0), (False, 32)):
        for i in range(2):
            k.op("t", lambda e, i=i, ncb=ncb: e.transpose(
                nrm["pn"].t[0:32, 0:128], ncol[i].t[:, ncb:ncb + 32],
                g["eyef"].ap()),
                reads=[ncol[i], g["eyef"]], writes=[nrm["pn"]])
            k.op("v", lambda e, i=i: e.reduce_max(
                nrm["col2"].t[0:32, 0:1], nrm["pn"].t[0:32, 0:128], axis=AX.X),
                reads=[nrm["pn"]], writes=[nrm["col2"]])
            k.op("t", lambda e, i=i: e.transpose(
                nrm["pn2"].t[0:1, 0:32], nrm["col2"].t[0:32, 0:1],
                g["eyef"].t[0:32, 0:32]),
                reads=[nrm["col2"], g["eyef"]], writes=[nrm["pn2"]])
            rowdst = nrm["rowq"] if isq else nrm["rowk"]
            for qb in range(8):
                k.op("v", lambda e, i=i, qb=qb, rowdst=rowdst: e.tensor_reduce(
                    rowdst.t[0:1, 8 * i + qb:8 * i + qb + 1],
                    nrm["pn2"].t[0:1, 4 * qb:4 * qb + 4], AX.X, ALU.max),
                    reads=[nrm["pn2"]], writes=[rowdst])


def _attn_head(k, g, sub, layer, h, i, q_ap, k_ap, qsrc, ksrc, vbuf, attnP,
               pe, pv, prep, preps, wT, ccol, bcol2, kposc, biasq, dtile, mC, rrow):
    NQB = 8
    pending = [None]

    def flush_norm():
        if pending[0] is not None:
            pending[0]()
            pending[0] = None

    for qb in range(NQB):
        pvx = pv[qb % 2]
        if layer == 0:
            kcs = list(range(32))
            k.op("v", lambda e, h=h, qb=qb: e.tensor_scalar(
                biasq.t[:, 32 * (qb % 2):32 * (qb % 2) + 32], kposc.ap(),
                ccol.t[:, h:h + 1],
                bcol2.t[:, 8 * h + qb:8 * h + qb + 1], ALU.mult, ALU.add),
                reads=[kposc, ccol, bcol2], writes=[biasq])
        else:
            kcs = list(range(4 * qb + 4))
        first_kc, last_kc = kcs[0], kcs[-1]
        nk = len(kcs)

        def emit_qk(kc, qb=qb):
            diag = (4 * qb <= kc < 4 * qb + 4)
            lower = kc < 4 * qb
            pex = pe[kc % 4]
            KK = 70 if (layer == 0 and (lower or diag)) else 64
            k.op("t", lambda e, kc=kc, KK=KK, pex=pex: e.matmul(
                pex.ap(), k_ap(KK, kc), q_ap(KK, qb), start=True, stop=True),
                reads=[ksrc, qsrc], writes=[pex])
            if diag:
                jj = kc - 4 * qb
                if layer == 0:
                    k.op("v", lambda e, jj=jj, pex=pex: e.scalar_tensor_tensor(
                        pex.ap(), dtile[jj].ap(), ccol.t[:, h:h + 1], pex.ap(),
                        ALU.mult, ALU.add),
                        reads=[dtile[jj], ccol, pex], writes=[pex])
                else:
                    k.op("v", lambda e, jj=jj, pex=pex: e.tensor_tensor(
                        pex.ap(), pex.ap(), dtile[jj].ap(), ALU.add),
                        reads=[pex, dtile[jj]], writes=[pex])

        def emit_exp(kc, qb=qb):
            lower = kc < 4 * qb
            pex = pe[kc % 4]
            wTx = wT[kc % 4]
            if layer == 0 and lower:
                bias_ap = biasq.t[:, 32 * (qb % 2) + kc:32 * (qb % 2) + kc + 1]
                bias_rd = [biasq]
            else:
                bias_ap, bias_rd = mC.t[:, 8 * i + qb:8 * i + qb + 1], [mC]
            k.op("s", lambda e, pex=pex, wTx=wTx, bias_ap=bias_ap: e.activation(
                wTx.ap(), pex.ap(), AF.Exp, bias=bias_ap),
                reads=[pex] + bias_rd, writes=[wTx])

        def emit_pv(kc, qb=qb, pvx=pvx, first_kc=first_kc, last_kc=last_kc):
            wTx = wT[kc % 4]
            k.op("t", lambda e, kc=kc, wTx=wTx: e.matmul(
                pvx.ap(), vbuf.t[:, 130 * kc + 65 * i:130 * kc + 65 * (i + 1)],
                wTx.ap(), start=(kc == first_kc), stop=(kc == last_kc)),
                reads=[vbuf, wTx],
                writes=[pvx] if (kc == first_kc or kc == last_kc) else [],
                inc=(kc == last_kc))

        for idx in range(nk + 2):
            if idx < nk:
                emit_qk(kcs[idx])
            if idx == 3:
                flush_norm()
            if 1 <= idx < nk + 1:
                emit_exp(kcs[idx - 1])
            if idx >= 2:
                emit_pv(kcs[idx - 2])
        if nk + 2 <= 3:
            flush_norm()

        def do_norm(qb=qb, pvx=pvx):
            k.op("v", lambda e, pvx=pvx: e.reciprocal(rrow.ap(), pvx.t[64:65, :]),
                 reads=[pvx], writes=[rrow])
            k.op("t", lambda e: e.matmul(prep.ap(), g["ones_row_f"].t[0:1, 0:65],
                                         rrow.ap(), start=True, stop=True),
                 reads=[g["ones_row_f"], rrow], writes=[prep])
            k.op("v", lambda e: e.tensor_copy(preps.ap(), prep.ap()),
                 reads=[prep], writes=[preps])
            k.op("v", lambda e, qb=qb, pvx=pvx: e.tensor_tensor(
                attnP.t[64 * i:64 * i + 64,
                        4096 * (h // 2) + 512 * qb:4096 * (h // 2) + 512 * (qb + 1)],
                pvx.t[0:64, :], preps.t[0:64, :], ALU.mult),
                reads=[pvx, preps], writes=[attnP])
        pending[0] = do_norm
    flush_norm()


def _build_rope_tables(k, es, g, sub, cost, sint):
    with ExitStack() as tes:
        invf = k.sb(tes, f"s{sub}_invf", [1, 32], F32)
        invr = k.sb(tes, f"s{sub}_invr", [128, 32], F32)
        posc = k.sb(tes, f"s{sub}_posc", [128, 32], F32)
        frq = k.sb(tes, f"s{sub}_frq", [128, 512], F32)
        red = k.sb(tes, f"s{sub}_red", [128, 256], F32)
        z1 = k.sb(tes, f"s{sub}_rz1", [128, 256], I16)
        half = k.sb(tes, f"s{sub}_half", [128, 256], F32)
        sc = k.sb(tes, f"s{sub}_sc", [128, 1], F32)
        bc = k.sb(tes, f"s{sub}_bc", [128, 1], F32)
        pc = k.sb(tes, f"s{sub}_pc", [128, 1], F32)
        pbi = k.ps(tes, f"s{sub}_pbi", [128, 32])
        LN1E4_32 = float(np.log(10000.0) / 32.0)
        INV2PI = float(1.0 / (2.0 * np.pi))
        C1 = float(np.float32(6.28125))
        C2 = float(np.float32(2.0 * np.pi - 6.28125))
        k.op("g", lambda e: e.iota(invf.ap(), pattern=[[1, 32]], base=0,
                                   channel_multiplier=0,
                                   allow_small_or_imprecise_dtypes=True),
             writes=[invf])
        k.op("v", lambda e: e.memset(sc.ap(), -LN1E4_32), writes=[sc])
        k.op("s", lambda e: e.activation(invf.ap(), invf.ap(), AF.Exp,
                                         scale=sc.t[0:1, :]),
             reads=[invf, sc], writes=[invf])
        k.op("t", lambda e: e.matmul(pbi.ap(), g["ones_row_f"].ap(), invf.ap(),
                                     start=True, stop=True),
             reads=[g["ones_row_f"], invf], writes=[pbi])
        k.op("v", lambda e: e.tensor_copy(invr.ap(), pbi.ap()),
             reads=[pbi], writes=[invr])
        k.op("g", lambda e: e.iota(posc.ap(), pattern=[[128, 32]], base=0,
                                   channel_multiplier=1,
                                   allow_small_or_imprecise_dtypes=True),
             writes=[posc])
        k.op("v", lambda e: e.memset(sc.ap(), INV2PI), writes=[sc])
        for hv in range(2):
            for t_ in range(16):
                k.op("v", lambda e, hv=hv, t_=t_: e.tensor_scalar(
                    frq.t[:, 32 * t_:32 * (t_ + 1)], invr.ap(),
                    posc.t[:, 16 * hv + t_:16 * hv + t_ + 1], None, ALU.mult),
                    reads=[invr, posc], writes=[frq])
            for table, phase in ((sint, 0.0), (cost, float(np.pi / 2))):
                k.op("v", lambda e, phase=phase: e.memset(
                    bc.ap(), 16384.0 + phase * INV2PI), writes=[bc])
                k.op("v", lambda e, phase=phase: e.memset(pc.ap(), phase),
                     writes=[pc])
                for chk in range(2):
                    fap = frq.t[:, 256 * chk:256 * (chk + 1)]
                    k.op("s", lambda e, fap=fap: e.activation(
                        red.ap(), fap, AF.Identity,
                        bias=bc.ap(), scale=sc.ap()),
                        reads=[frq, sc, bc], writes=[red])
                    k.op("v", lambda e: e.tensor_copy(z1.ap(), red.ap()),
                         reads=[red], writes=[z1])
                    k.op("v", lambda e: e.scalar_tensor_tensor(
                        red.ap(), z1.ap(), 0.5, red.ap(), ALU.add, ALU.is_le),
                        reads=[z1, red], writes=[red])
                    k.op("v", lambda e: e.scalar_tensor_tensor(
                        red.ap(), red.ap(), -16384.0, z1.ap(), ALU.add, ALU.add),
                        reads=[red, z1], writes=[red])
                    k.op("s", lambda e, fap=fap: e.activation(
                        half.ap(), fap, AF.Identity, bias=pc.ap(), scale=1.0),
                        reads=[frq, pc], writes=[half])
                    k.op("v", lambda e: e.scalar_tensor_tensor(
                        half.ap(), red.ap(), -C1, half.ap(), ALU.mult, ALU.add),
                        reads=[red, half], writes=[half])
                    k.op("v", lambda e: e.scalar_tensor_tensor(
                        half.ap(), red.ap(), -C2, half.ap(), ALU.mult, ALU.add),
                        reads=[red, half], writes=[half])
                    k.op("v", lambda e: e.tensor_scalar(
                        half.ap(), half.ap(), float(np.pi - 1e-6),
                        float(-(np.pi - 1e-6)), ALU.min, ALU.max),
                        reads=[half], writes=[half])
                    k.op("s", lambda e: e.activation(half.ap(), half.ap(),
                                                     AF.Sin),
                         reads=[half], writes=[half])
                    k.op("v", lambda e, table=table, hv=hv, chk=chk:
                         e.tensor_copy(
                             _strided(table.ap(), 1024 * hv + 512 * chk,
                                      [[64, 8], [1, 32]]),
                             _strided(half.ap(), 0, [[32, 8], [1, 32]])),
                         reads=[half], writes=[table])
                    k.op("v", lambda e, table=table, hv=hv, chk=chk:
                         e.tensor_copy(
                             _strided(table.ap(), 1024 * hv + 512 * chk + 32,
                                      [[64, 8], [1, 32]]),
                             _strided(half.ap(), 0, [[32, 8], [1, 32]])),
                         reads=[half], writes=[table])
        k.op("v", lambda e: e.tensor_scalar(
            _strided(sint.ap(), 0, [[64, 32], [1, 32]]),
            _strided(sint.ap(), 0, [[64, 32], [1, 32]]), -1.0, None, ALU.mult),
            reads=[sint], writes=[sint])
        k.join()


_NC_CACHE = {}


def _get_nc(nsub):
    if nsub not in _NC_CACHE:
        _NC_CACHE[nsub] = build_program(nsub)
    return _NC_CACHE[nsub]


def kernel(L0_attn_x, L0_attn_qkv_w, L0_attn_proj_w, L0_attn_proj_b,
           L0_attn_slopes, L0_ffn_w1, L0_ffn_conv_w, L0_ffn_conv_b, L0_ffn_w2,
           L1_attn_w_q, L1_attn_w_k, L1_attn_w_v, L1_attn_w_o,
           L1_ffn_router_w, L1_ffn_router_b, L1_ffn_expert_w, L1_ffn_expert_b):
    nsub = N_SUBLAYERS
    nc = _get_nc(nsub)

    x = np.ascontiguousarray(np.asarray(L0_attn_x, dtype=np.float32))
    w1t = np.ascontiguousarray(np.asarray(L0_ffn_w1, np.float32).T)
    w2t = np.ascontiguousarray(np.asarray(L0_ffn_w2, np.float32).T)
    convw = np.ascontiguousarray(np.asarray(L0_ffn_conv_w, np.float32)
                                 .reshape(NCT, 128, 3).transpose(1, 0, 2).reshape(128, 3 * NCT))
    convb = np.ascontiguousarray(np.asarray(L0_ffn_conv_b, np.float32)
                                 .reshape(NCT, 128).T)
    qkv_w = np.asarray(L0_attn_qkv_w, np.float32)
    proj_wt = np.ascontiguousarray(np.asarray(L0_attn_proj_w, np.float32).T)
    proj_b = np.asarray(L0_attn_proj_b, np.float32)
    slopes = np.asarray(L0_attn_slopes, np.float32)
    wq_l1 = np.asarray(L1_attn_w_q, np.float32)
    wk_l1 = np.asarray(L1_attn_w_k, np.float32)
    wv_l1 = np.asarray(L1_attn_w_v, np.float32)
    wo_l1 = np.asarray(L1_attn_w_o, np.float32)
    router_w = np.asarray(L1_ffn_router_w, np.float32)
    router_b = np.asarray(L1_ffn_router_b, np.float32)
    expert_w = np.asarray(L1_ffn_expert_w, np.float32)
    expert_b = np.asarray(L1_ffn_expert_b, np.float32)

    routerw_s = np.ascontiguousarray(
        np.concatenate([router_w, router_b[None, :]], axis=0))
    expertw_s = np.ascontiguousarray(
        np.concatenate([expert_w, expert_b[:, None, :]], axis=1))

    in_maps = []
    for c in range(8):
        b, hf = c // 2, c % 2
        r0 = hf * TOK
        heads = range(8 * hf, 8 * hf + 8)
        halo = np.zeros((2, C), np.float32)
        if r0 - 1 >= 0:
            halo[0] = x[b, r0 - 1]
        if r0 + TOK < N:
            halo[1] = x[b, r0 + TOK]
        qsl = np.concatenate([qkv_w[64 * h:64 * h + 64] for h in heads], 0)
        ksl = np.concatenate([qkv_w[1024 + 64 * h:1024 + 64 * h + 64] for h in heads], 0)
        vsl = np.concatenate([qkv_w[2048 + 64 * h:2048 + 64 * h + 64] for h in heads], 0)
        qkvwt_s = np.ascontiguousarray(np.concatenate([qsl, ksl, vsl], 0).T)
        def stage_proj(wt_in, bias):
            blocks = [wt_in[64 * hh:64 * hh + 64] for hh in heads]
            blocks.append(bias[None, :])
            return np.ascontiguousarray(np.concatenate(blocks, 0))
        projwt_s = stage_proj(proj_wt, proj_b if hf == 0 else np.zeros_like(proj_b))
        q1 = np.concatenate([wq_l1[h] for h in heads], 1)
        k1 = np.concatenate([wk_l1[h] for h in heads], 1)
        v1 = np.concatenate([wv_l1[h] for h in heads], 1)
        l1qkvt_s = np.ascontiguousarray(np.concatenate([q1, k1, v1], 1))
        l1wot_s = stage_proj(wo_l1, np.zeros(C, np.float32))
        metav = np.zeros((1, 8), np.float32)
        metav[0, 0] = 1.0 if hf == 0 else 0.0
        metav[0, 1] = 1.0 if hf == 1 else 0.0
        in_maps.append({
            "x_own": np.ascontiguousarray(x[b, r0:r0 + TOK]),
            "x_halo": halo,
            "w1t": w1t, "w2t": w2t, "convw": convw, "convb": convb,
            "qkvwt": qkvwt_s, "projwt": projwt_s,
            "slopes8": np.ascontiguousarray(slopes[8 * hf:8 * hf + 8][None, :]),
            "l1qkvt": l1qkvt_s, "l1wot": l1wot_s,
            "routerw": routerw_s, "expertw": expertw_s,
            "meta": metav,
        })

    res = run_bass_kernel_spmd(nc, in_maps, core_ids=list(range(8)))
    out = np.empty((B, N, C), np.float32)
    for c in range(8):
        b, hf = c // 2, c % 2
        out[b, hf * TOK:(hf + 1) * TOK] = res.results[c]["out"]
    return out


# version 17
# speedup vs baseline: 1.0145x; 1.0145x over previous
import os
import numpy as np
import ml_dtypes
import dataclasses as _dc
from contextlib import ExitStack

import concourse.bass as bass
import concourse.mybir as mybir
from concourse.bass_utils import run_bass_kernel_spmd

F32 = mybir.dt.float32
BF16 = mybir.dt.bfloat16
F16 = mybir.dt.float16
I16 = mybir.dt.int16
AF = mybir.ActivationFunctionType
ALU = mybir.AluOpType
AX = mybir.AxisListType

B, N, C = 4, 4096, 1024
TOK = 2048
NT = TOK // 128
DH = 4096
NCT = DH // 128
TH = 2052
NH = 8
HD = 64
NE = 8
EPS = 1e-5
PI = float(np.pi)

N_SUBLAYERS = int(os.environ.get("MK_NSUB", "6"))
SUB_ORDER = [int(x) for x in os.environ.get("MK_ORDER", "0,1,2,3,4,5").split(",")]
RG = [[0, 1], [2, 3], [4, 5], [6, 7]]


def _strided(apv, offset, dims):
    return _dc.replace(apv, offset=offset, ap=[list(apv.ap[0])] + [list(d) for d in dims])


def _apfull(apv, offset, dims):
    return _dc.replace(apv, offset=offset, ap=[list(d) for d in dims])


class Buf:
    __slots__ = ("t", "last_w", "readers")

    def __init__(self, t):
        self.t = t
        self.last_w = None
        self.readers = []

    def ap(self):
        return self.t[:]


DMA_POOL = {"q": 8, "gd": 4, "o": 2, "sd": 2, "vd": 2}


class K:
    def __init__(self, nc, es):
        self.nc = nc
        self.es = es
        self.eng = {"v": nc.vector, "s": nc.scalar, "t": nc.tensor,
                    "g": nc.gpsimd, "q": nc.sync}
        self.sem = {}
        self.cnt = {}
        self.step = {"v": 1, "s": 1, "t": 1, "g": 1, "q": 16, "gd": 16, "cc": 1,
                     "o": 16, "sd": 16, "vd": 16}
        for name in list(self.step):
            if name in DMA_POOL:
                for i in range(DMA_POOL[name]):
                    nm = f"{name}{i}"
                    self.sem[nm] = es.enter_context(nc.semaphore("sem_" + nm))
                    self.cnt[nm] = 0
                    self.step[nm] = 16
            else:
                self.sem[name] = es.enter_context(nc.semaphore("sem_" + name))
                self.cnt[name] = 0
        self.rr = {name: 0 for name in DMA_POOL}
        self.wm = {e: {} for e in self.eng}
        self.pend = {e: [] for e in self.eng}

    @staticmethod
    def _e(ch):
        return {"gd": "g", "cc": "g", "o": "q", "sd": "s", "vd": "v"}.get(ch, ch)

    def wait(self, ch, tok):
        e = self._e(ch)
        s, v = tok
        if self.wm[e].get(s, -1) >= v:
            return
        self.eng[e].wait_ge(self.sem[s], v)
        self.wm[e][s] = v

    def op(self, ch, emit, reads=(), writes=(), inc=True):
        e = self._e(ch)
        if ch in DMA_POOL:
            i = self.rr[ch]
            self.rr[ch] = (i + 1) % DMA_POOL[ch]
            sch = f"{ch}{i}"
        else:
            sch = ch
        best = {}
        for b in reads:
            if b.last_w is not None:
                s, v = b.last_w
                best[s] = max(best.get(s, -1), v)
        for b in writes:
            toks = ([b.last_w] if b.last_w is not None else []) + b.readers
            for s, v in toks:
                best[s] = max(best.get(s, -1), v)
        if sch != ch and self.cnt[sch] > 0:
            best[sch] = max(best.get(sch, -1), self.cnt[sch])
        for s, v in best.items():
            self.wait(ch, (s, v))
        instr = emit(self.eng[e])
        if inc:
            self.cnt[sch] += self.step[sch]
            instr.then_inc(self.sem[sch], self.step[sch])
            tok = (sch, self.cnt[sch])
            for b in writes:
                b.last_w = tok
                b.readers = []
            for b in reads:
                b.readers.append(tok)
            for b in self.pend[e]:
                b.readers.append(tok)
            self.pend[e] = []
        else:
            for b in reads:
                self.pend[e].append(b)
        return instr

    def join(self):
        for e in self.eng:
            for ch in self.cnt:
                if self.cnt[ch] > 0:
                    self.wait(e, (ch, self.cnt[ch]))

    def sb(self, es, name, shape, dtype):
        return Buf(es.enter_context(self.nc.sbuf_tensor(name, list(shape), dtype)))

    def ps(self, es, name, shape, dtype=F32):
        return Buf(es.enter_context(self.nc.psum_tensor(name, list(shape), dtype)))


def _round_chain(k, ybuf, yap, z1buf, z1ap, outbuf, outap, clip1=False, cv_ch="v"):
    if cv_ch == "s":
        k.op("s", lambda e: e.activation(z1ap, yap, AF.Copy), reads=[ybuf], writes=[z1buf])
    else:
        k.op("v", lambda e: e.tensor_copy(z1ap, yap), reads=[ybuf], writes=[z1buf])
    k.op("v", lambda e: e.scalar_tensor_tensor(yap, z1ap, 0.5, yap, ALU.add, ALU.is_le),
         reads=[z1buf, ybuf], writes=[ybuf])
    if clip1:
        k.op("v", lambda e: e.scalar_tensor_tensor(yap, yap, -16384.0, z1ap, ALU.add, ALU.add),
             reads=[ybuf, z1buf], writes=[ybuf])
        k.op("v", lambda e: e.tensor_scalar(outap, yap, 1.0, -1.0, ALU.min, ALU.max),
             reads=[ybuf], writes=[outbuf])
    else:
        k.op("v", lambda e: e.scalar_tensor_tensor(outap, yap, -16384.0, z1ap, ALU.add, ALU.add),
             reads=[ybuf, z1buf], writes=[outbuf])


def build_program(nsub):
    nc = bass.Bass("TRN2", target_bir_lowering=False, debug=False)

    def din(name, shape, dtype=F32):
        return Buf(nc.dram_tensor(name, list(shape), dtype, kind="ExternalInput").ap())

    def dint(name, shape, dtype):
        return Buf(nc.dram_tensor(name, list(shape), dtype, kind="Internal").ap())

    x_own = din("x_own", [TOK, C])
    x_halo = din("x_halo", [2, C])
    w1t = din("w1t", [C, DH])
    w2t = din("w2t", [DH, C])
    convw = din("convw", [128, 3 * NCT])
    convb = din("convb", [128, NCT])
    qkvwt = din("qkvwt", [C, 3 * 512])
    projwt = din("projwt", [513, C])
    slopes8 = din("slopes8", [1, NH])
    l1qkvt = din("l1qkvt", [C, 3 * 512])
    l1wot = din("l1wot", [513, C])
    routerw = din("routerw", [C + 1, NE])
    expertw = din("expertw", [NE, C + 1, C])
    meta = din("meta", [1, 8])
    out_t = Buf(nc.dram_tensor("out", [TOK, C], F32, kind="ExternalOutput").ap())

    xres = dint("xres", [TOK, C], F32)
    wq1c = dint("wq1c", [8, 128, DH], BF16)
    wq2c = dint("wq2c", [NCT, 128, C], BF16)
    ag_in = dint("ag_in", [8, 128, TOK], BF16)
    ag_out = dint("ag_out", [2, 8, 128, TOK], BF16)
    rs_in = dint("rs_in", [4, 2, 512, C], F32)
    rs_out = dint("rs_out", [4, 512, C], F32)
    eh_in = dint("eh_in", [2, NCT, 128, 2], F32)
    eh_out = dint("eh_out", [NCT, 128, 2], F32)

    es = ExitStack()
    with es:
        k = K(nc, es)
        g = {}

        g["eye"] = k.sb(es, "g_eye", [128, 128], BF16)
        g["ones_row_f"] = k.sb(es, "g_onesrowf", [1, 128], F32)
        g["ones_col_f"] = k.sb(es, "g_onescolf", [128, 1], F32)

        g["meta"] = k.sb(es, "g_meta", [1, 8], F32)
        g["eps"] = k.sb(es, "g_eps", [128, 1], F32)
        g["ones_row_b16"] = k.sb(es, "g_onesrowb16", [1, 512], BF16)
        g["eyef"] = k.sb(es, "g_eyef", [128, 128], F32)
        k.op("g", lambda e: e.memset(g["eye"].ap(), 1.0), writes=[g["eye"]])
        k.op("g", lambda e: e.affine_select(g["eye"].ap(), g["eye"].ap(),
                                            pattern=[[-1, 128]], compare_op=ALU.is_equal,
                                            fill=0.0, base=0, channel_multiplier=1),
             reads=[g["eye"]], writes=[g["eye"]])
        k.op("g", lambda e: e.memset(g["ones_row_f"].ap(), 1.0), writes=[g["ones_row_f"]])
        k.op("g", lambda e: e.memset(g["ones_col_f"].ap(), 1.0), writes=[g["ones_col_f"]])

        k.op("g", lambda e: e.memset(g["eps"].ap(), EPS), writes=[g["eps"]])
        k.op("g", lambda e: e.memset(g["ones_row_b16"].ap(), 1.0), writes=[g["ones_row_b16"]])
        k.op("g", lambda e: e.memset(g["eyef"].ap(), 1.0), writes=[g["eyef"]])
        k.op("g", lambda e: e.affine_select(g["eyef"].ap(), g["eyef"].ap(),
                                            pattern=[[-1, 128]], compare_op=ALU.is_equal,
                                            fill=0.0, base=0, channel_multiplier=1),
             reads=[g["eyef"]], writes=[g["eyef"]])
        k.op("q", lambda e: e.dma_start(out=g["meta"].ap(), in_=meta.ap()),
             reads=[meta], writes=[g["meta"]])

        conv_cols = _load_conv(k, es, convw, convb)
        g["swinv"] = k.sb(es, "g_swinv", [128, 2], F32)
        with ExitStack() as pes:
            _prep_weights(k, pes, g, w1t, w2t, wq1c, wq2c)
        k.join()

        if True:
            for si in range(nsub):
                sub = SUB_ORDER[si]
                layer = 0 if sub < 3 else 1
                kind = ("ffn", "attn", "ffn")[sub % 3]
                xsrc = x_own if si == 0 else xres
                xdst = out_t if si == nsub - 1 else xres
                if kind == "ffn" and layer == 0:
                    _l0_ffn(k, g, sub, xsrc, xdst, x_halo, wq1c, wq2c,
                            conv_cols, eh_in, eh_out, first=(si == 0))
                elif kind == "ffn":
                    _moe(k, g, sub, xsrc, xdst, routerw, expertw)
                else:
                    _attention(k, g, sub, layer, xsrc, xdst,
                               qkvwt if layer == 0 else l1qkvt,
                               projwt if layer == 0 else l1wot,
                               slopes8, ag_in, ag_out, rs_in, rs_out)
                k.join()

        k.join()
        for i in range(DMA_POOL["o"]):
            nm = f"o{i}"
            if k.cnt[nm] > 0:
                k.eng["q"].wait_ge(k.sem[nm], k.cnt[nm])
    return nc


def _prep_weights(k, es, g, w1t, w2t, wq1c, wq2c):
    stage = [k.sb(es, f"wp_st{i}", [128, DH], F32) for i in range(6)]
    qstage = [k.sb(es, f"wp_q{i}", [128, DH], BF16) for i in range(2)]
    z1 = [k.sb(es, f"wp_z1{i}", [128, DH], I16) for i in range(2)]
    acc = k.sb(es, "wp_acc", [128, 8 + NCT], F32)
    tot = k.sb(es, "wp_tot", [128, 2], F32)
    m2 = k.sb(es, "wp_m2", [1, 2], F32)
    swrow = k.sb(es, "wp_swrow", [1, 2], F32)
    swc = k.sb(es, "wp_swc", [128, 2], F32)
    minv = k.sb(es, "wp_minv", [128, 2], F32)
    pcol = k.ps(es, "wp_pcol", [1, 2])
    pbc = k.ps(es, "wp_pbc", [128, 2])

    specs = [(w1t, 8, DH, wq1c, 0), (w2t, NCT, C, wq2c, 1)]
    for wt, npt, width, _, wi in specs:
        for i in range(npt):
            sbuf = stage[i % 6]
            col = i if wi == 0 else 8 + i
            hw = width // 2
            for h2, ch in ((0, ("q", "gd", "sd")[i % 3]),
                           (1, ("gd", "sd", "q")[i % 3])):
                k.op(ch, lambda e, wt=wt, i=i, hw=hw, h2=h2, sbuf=sbuf: e.dma_start(
                    out=sbuf.t[:, h2 * hw:(h2 + 1) * hw],
                    in_=wt.t[i * 128:(i + 1) * 128, h2 * hw:(h2 + 1) * hw]),
                    reads=[wt], writes=[sbuf])
            k.op("v", lambda e, sbuf=sbuf, col=col, width=width: e.reduce_sum(
                acc.t[:, col:col + 1], sbuf.t[:, 0:width], axis=AX.X,
                apply_absolute_value=True),
                reads=[sbuf], writes=[acc])
    k.op("v", lambda e: e.reduce_sum(tot.t[:, 0:1], acc.t[:, 0:8], axis=AX.X),
         reads=[acc], writes=[tot])
    k.op("v", lambda e: e.reduce_sum(tot.t[:, 1:2], acc.t[:, 8:8 + NCT], axis=AX.X),
         reads=[acc], writes=[tot])
    k.op("t", lambda e: e.matmul(pcol.ap(), g["ones_col_f"].ap(), tot.ap(),
                                 start=True, stop=True),
         reads=[g["ones_col_f"], tot], writes=[pcol])
    k.op("v", lambda e: e.tensor_scalar(m2.ap(), pcol.ap(), 1.0 / (DH * C), 1e-5,
                                        ALU.mult, ALU.max),
         reads=[pcol], writes=[m2])
    k.op("v", lambda e: e.reciprocal(swrow.ap(), m2.ap()), reads=[m2], writes=[swrow])
    k.op("t", lambda e: e.matmul(pbc.ap(), g["ones_row_f"].ap(), swrow.ap(),
                                 start=True, stop=True),
         reads=[g["ones_row_f"], swrow], writes=[pbc])
    k.op("v", lambda e: e.tensor_copy(swc.ap(), pbc.ap()), reads=[pbc], writes=[swc])
    k.op("t", lambda e: e.matmul(pbc.ap(), g["ones_row_f"].ap(), m2.ap(),
                                 start=True, stop=True),
         reads=[g["ones_row_f"], m2], writes=[pbc])
    k.op("v", lambda e: e.tensor_copy(minv.ap(), pbc.ap()), reads=[pbc], writes=[minv])
    k.op("v", lambda e: e.tensor_copy(g["swinv"].ap(), minv.ap()),
         reads=[minv], writes=[g["swinv"]])

    for wt, npt, width, cache, wi in specs:
        for i in range(npt):
            sbuf = stage[i % 6]
            zb, qs = z1[i % 2], qstage[i % 2]
            hw = width // 2
            for h2, ch in ((0, ("q", "gd", "sd")[i % 3]),
                           (1, ("gd", "sd", "q")[i % 3])):
                k.op(ch, lambda e, wt=wt, i=i, hw=hw, h2=h2, sbuf=sbuf: e.dma_start(
                    out=sbuf.t[:, h2 * hw:(h2 + 1) * hw],
                    in_=wt.t[i * 128:(i + 1) * 128, h2 * hw:(h2 + 1) * hw]),
                    reads=[wt], writes=[sbuf])
            k.op("s", lambda e, sbuf=sbuf, wi=wi, width=width: e.activation(
                sbuf.t[:, 0:width], sbuf.t[:, 0:width], AF.Copy,
                bias=16384.0, scale=swc.t[:, wi:wi + 1]),
                reads=[sbuf, swc], writes=[sbuf])
            _round_chain(k, sbuf, sbuf.t[:, 0:width], zb, zb.t[:, 0:width],
                         qs, qs.t[:, 0:width], clip1=True, cv_ch="s")
            k.op("gd" if i % 2 == 0 else "q",
                 lambda e, cache=cache, i=i, width=width, qs=qs: e.dma_start(
                     out=cache.t[i, :, 0:width], in_=qs.t[:, 0:width]),
                 reads=[qs], writes=[cache])
    k.join()


def _load_conv(k, es, convw, convb):
    cw = k.sb(es, "g_convw", [128, 3 * NCT], F32)
    cb = k.sb(es, "g_convb", [128, NCT], F32)
    k.op("q", lambda e: e.dma_start(out=cw.ap(), in_=convw.ap()),
         reads=[convw], writes=[cw])
    k.op("q", lambda e: e.dma_start(out=cb.ap(), in_=convb.ap()),
         reads=[convb], writes=[cb])
    return {"w": cw, "b": cb}


def _ln_stats(k, cols, xb, P, quant):
    pass


def _l0_ffn(k, g, sub, xsrc, xdst, x_halo, wq1c, wq2c, conv_cols, eh_in, eh_out,
            first):
    nc = k.nc
    with ExitStack() as es:
        xqT = k.sb(es, f"s{sub}_xqT", [128, 8 * TH], BF16)
        hb = [k.sb(es, f"s{sub}_h{ct}", [128, TH], BF16) for ct in range(NCT)]
        st1rep = k.sb(es, f"s{sub}_st1rep", [128, TH], BF16)
        st2rep = k.sb(es, f"s{sub}_st2rep", [128, TH], BF16)
        stcol = k.sb(es, f"s{sub}_stcol", [128, 32], F32)
        mxcol = k.sb(es, f"s{sub}_mxcol", [128, 32], F32)
        strow = k.sb(es, f"s{sub}_strow", [32, 128], F32)
        strowf = k.sb(es, f"s{sub}_strowf", [1, 18 * 128], BF16)
        st2col = k.sb(es, f"s{sub}_st2col", [128, 32], F32)
        st2i05 = k.sb(es, f"s{sub}_st2i05", [128, NT], F32)
        w1b = [k.sb(es, f"s{sub}_w1b{i}", [128, 8 * 128], BF16) for i in range(2)]

        sw1b = g["swinv"]

        with ExitStack() as aes:
            ptb = [k.ps(aes, f"s{sub}_ptb{i}", [128, 128], BF16) for i in range(2)]
            xt = [k.sb(aes, f"s{sub}_xt{i}", [128, C], F32) for i in range(2)]
            ysc = [k.sb(aes, f"s{sub}_ysc{i}", [128, C], F32) for i in range(2)]
            z1 = [k.sb(aes, f"s{sub}_z1{i}", [128, C], I16) for i in range(2)]
            xqs = [k.sb(aes, f"s{sub}_xqs{i}", [128, C], BF16) for i in range(2)]
            sdump = k.sb(aes, f"s{sub}_sdump", [128, C], BF16)
            colsb = [k.sb(aes, f"s{sub}_cols{i}", [128, 12], F32) for i in range(2)]

            def ln_quant_tile(tt, P, src_ap, srcbuf, halo):
                i = tt % 2
                xb, yb, zb, qb, cols = xt[i], ysc[i], z1[i], xqs[i], colsb[i]
                k.op("q", lambda e: e.dma_start(out=xb.t[0:P, :], in_=src_ap),
                     reads=[srcbuf], writes=[xb])
                k.op("v", lambda e: e.reduce_sum(cols.t[0:P, 0:1], xb.t[0:P, :], axis=AX.X),
                     reads=[xb], writes=[cols])
                k.op("s", lambda e: e.activation(sdump.t[0:P, :], xb.t[0:P, :], AF.Square,
                                                 accum_out=cols.t[0:P, 1:2]),
                     reads=[xb], writes=[sdump, cols])
                k.op("v", lambda e: e.reduce_max(cols.t[0:P, 5:6], xb.t[0:P, :], axis=AX.X),
                     reads=[xb], writes=[cols])
                k.op("v", lambda e: e.tensor_reduce(cols.t[0:P, 6:7], xb.t[0:P, :],
                                                    AX.X, ALU.min),
                     reads=[xb], writes=[cols])
                k.op("v", lambda e: e.tensor_scalar(cols.t[0:P, 2:3], cols.t[0:P, 0:1],
                                                    1.0 / C, None, ALU.mult),
                     reads=[cols], writes=[cols])
                k.op("v", lambda e: e.tensor_mul(cols.t[0:P, 3:4], cols.t[0:P, 2:3],
                                                 cols.t[0:P, 2:3]),
                     reads=[cols], writes=[cols])
                k.op("v", lambda e: e.scalar_tensor_tensor(cols.t[0:P, 3:4],
                                                           cols.t[0:P, 1:2], 1.0 / C,
                                                           cols.t[0:P, 3:4],
                                                           ALU.mult, ALU.subtract),
                     reads=[cols], writes=[cols])
                k.op("s", lambda e: e.activation(cols.t[0:P, 4:5], cols.t[0:P, 3:4],
                                                 AF.Sqrt, bias=g["eps"].t[0:P, :]),
                     reads=[cols, g["eps"]], writes=[cols])
                k.op("v", lambda e: e.reciprocal(cols.t[0:P, 4:5], cols.t[0:P, 4:5]),
                     reads=[cols], writes=[cols])
                k.op("v", lambda e: e.tensor_sub(cols.t[0:P, 5:6], cols.t[0:P, 5:6],
                                                 cols.t[0:P, 2:3]),
                     reads=[cols], writes=[cols])
                k.op("v", lambda e: e.tensor_sub(cols.t[0:P, 6:7], cols.t[0:P, 2:3],
                                                 cols.t[0:P, 6:7]),
                     reads=[cols], writes=[cols])
                k.op("v", lambda e: e.tensor_max(cols.t[0:P, 7:8], cols.t[0:P, 5:6],
                                                 cols.t[0:P, 6:7]),
                     reads=[cols], writes=[cols])
                k.op("v", lambda e: e.tensor_mul(cols.t[0:P, 7:8], cols.t[0:P, 7:8],
                                                 cols.t[0:P, 4:5]),
                     reads=[cols], writes=[cols])
                k.op("v", lambda e: e.tensor_scalar(cols.t[0:P, 7:8], cols.t[0:P, 7:8],
                                                    1e-5, None, ALU.max),
                     reads=[cols], writes=[cols])
                k.op("v", lambda e: e.reciprocal(cols.t[0:P, 8:9], cols.t[0:P, 7:8]),
                     reads=[cols], writes=[cols])
                k.op("v", lambda e: e.tensor_scalar(cols.t[0:P, 8:9], cols.t[0:P, 8:9],
                                                    127.0, None, ALU.mult),
                     reads=[cols], writes=[cols])
                k.op("v", lambda e: e.tensor_mul(cols.t[0:P, 8:9], cols.t[0:P, 8:9],
                                                 cols.t[0:P, 4:5]),
                     reads=[cols], writes=[cols])
                k.op("v", lambda e: e.tensor_mul(cols.t[0:P, 9:10], cols.t[0:P, 2:3],
                                                 cols.t[0:P, 8:9]),
                     reads=[cols], writes=[cols])
                k.op("v", lambda e: e.tensor_scalar(cols.t[0:P, 9:10], cols.t[0:P, 9:10],
                                                    -1.0, 16384.0, ALU.mult, ALU.add),
                     reads=[cols], writes=[cols])
                k.op("v", lambda e: e.scalar_tensor_tensor(
                    cols.t[0:P, 10:11], cols.t[0:P, 7:8], 1.0 / 127.0,
                    sw1b.t[0:P, 0:1], ALU.mult, ALU.mult),
                    reads=[cols, sw1b], writes=[cols])
                k.op("s", lambda e: e.activation(yb.t[0:P, :], xb.t[0:P, :], AF.Identity,
                                                 bias=cols.t[0:P, 9:10],
                                                 scale=cols.t[0:P, 8:9]),
                     reads=[xb, cols], writes=[yb])
                _round_chain(k, yb, yb.t[0:P, :], zb, zb.t[0:P, :], qb, qb.t[0:P, :])
                dstc = 17 if halo else tt
                k.op("v", lambda e: e.tensor_copy(stcol.t[0:P, dstc:dstc + 1],
                                                  cols.t[0:P, 10:11]),
                     reads=[cols], writes=[stcol])
                for j in range(8):
                    pb = ptb[j % 2]
                    k.op("t", lambda e, j=j: e.transpose(
                        pb.t[:, 0:P], qb.t[0:P, j * 128:(j + 1) * 128],
                        g["eye"].t[0:P, 0:P]),
                        reads=[qb, g["eye"]], writes=[pb])
                    if not halo:
                        k.op("v", lambda e, j=j, pb=pb: e.tensor_copy(
                            xqT.t[:, TH * j + 1 + tt * 128: TH * j + 1 + (tt + 1) * 128],
                            pb.t[:, 0:128]),
                            reads=[pb], writes=[xqT])
                    else:
                        k.op("v", lambda e, j=j, pb=pb: e.tensor_copy(
                            xqT.t[:, TH * j: TH * j + 1], pb.t[:, 0:1]),
                            reads=[pb], writes=[xqT])
                        k.op("v", lambda e, j=j, pb=pb: e.tensor_copy(
                            xqT.t[:, TH * j + 2049: TH * j + 2050], pb.t[:, 1:2]),
                            reads=[pb], writes=[xqT])

            for j in range(8):
                k.op("v", lambda e, j=j: e.memset(xqT.t[:, TH * j + 2050: TH * (j + 1)], 0.0),
                     writes=[xqT])
            k.op("v", lambda e: e.memset(stcol.ap(), 0.0), writes=[stcol])
            for tt in range(NT):
                ln_quant_tile(tt, 128, xsrc.t[tt * 128:(tt + 1) * 128, :], xsrc, False)
            if first:
                ln_quant_tile(16, 2, x_halo.ap(), x_halo, True)
            else:
                for j in range(8):
                    k.op("v", lambda e, j=j: e.memset(xqT.t[:, TH * j: TH * j + 1], 0.0),
                         writes=[xqT])
                    k.op("v", lambda e, j=j: e.memset(
                        xqT.t[:, TH * j + 2049: TH * j + 2050], 0.0),
                        writes=[xqT])
                k.op("v", lambda e: e.memset(stcol.t[:, 17:18], 0.0), writes=[stcol])

            prow = k.ps(aes, f"s{sub}_prow", [128, 128])
            pst = k.ps(aes, f"s{sub}_pst", [32, 128])
            k.op("t", lambda e: e.transpose(pst.ap(), stcol.ap(), g["eyef"].ap()),
                 reads=[stcol, g["eyef"]], writes=[pst])
            k.op("v", lambda e: e.tensor_copy(strow.ap(), pst.ap()),
                 reads=[pst], writes=[strow])
            k.op("gd", lambda e: e.dma_start(
                out=_strided(strowf.ap(), 0, [[128, 18], [1, 128]]),
                in_=strow.t[0:18, :]),
                reads=[strow], writes=[strowf])
            for tt in range(NT):
                k.op("t", lambda e, tt=tt: e.matmul(prow.ap(), g["ones_row_b16"].t[0:1, 0:128],
                                                    strowf.t[:, tt * 128:(tt + 1) * 128],
                                                    start=True, stop=True),
                     reads=[g["ones_row_b16"], strowf], writes=[prow])
                k.op("v", lambda e, tt=tt: e.tensor_copy(
                    st1rep.t[:, 1 + tt * 128:1 + (tt + 1) * 128], prow.ap()),
                    reads=[prow], writes=[st1rep])
            k.op("t", lambda e: e.matmul(prow.t[:, 0:2], g["ones_row_b16"].t[0:1, 0:128],
                                         strowf.t[:, 17 * 128:17 * 128 + 2],
                                         start=True, stop=True),
                 reads=[g["ones_row_b16"], strowf], writes=[prow])
            k.op("v", lambda e: e.tensor_copy(st1rep.t[:, 0:1], prow.t[:, 0:1]),
                 reads=[prow], writes=[st1rep])
            k.op("v", lambda e: e.tensor_copy(st1rep.t[:, 2049:2050], prow.t[:, 1:2]),
                 reads=[prow], writes=[st1rep])
            k.op("v", lambda e: e.memset(st1rep.t[:, 2050:TH], 0.0), writes=[st1rep])
            k.join()

        hrecv = None
        if not first:
            hrecv = k.sb(es, f"s{sub}_hrecv", [128, 2 * NCT], BF16)
            with ExitStack() as bes:
                pe = k.ps(bes, f"s{sub}_pedge", [128, 2])
                pfl = k.ps(bes, f"s{sub}_pfl", [128, 2])
                edge = k.sb(bes, f"s{sub}_edge", [128, 2 * NCT], F32)
                sedge = k.sb(bes, f"s{sub}_sedge", [128, 2 * NCT], F32)
                flg = k.sb(bes, f"s{sub}_flg", [128, 2], F32)
                k.op("t", lambda e: e.matmul(pfl.ap(), g["ones_row_f"].ap(),
                                             g["meta"].t[:, 0:2], start=True, stop=True),
                     reads=[g["ones_row_f"], g["meta"]], writes=[pfl])
                k.op("v", lambda e: e.tensor_copy(flg.ap(), pfl.ap()),
                     reads=[pfl], writes=[flg])
                for ct in range(NCT):
                    wb = w1b[ct % 2]
                    k.op("q", lambda e, ct=ct, wb=wb: e.dma_start(
                        out=_strided(wb.ap(), 0, [[128, 8], [1, 128]]),
                        in_=_apfull(wq1c.ap(), ct * 128,
                                    [[DH, 128], [128 * DH, 8], [1, 128]])),
                        reads=[wq1c], writes=[wb])
                    for ki in range(8):
                        rhs = _strided(xqT.ap(), TH * ki + 1, [[2047, 2]])
                        k.op("t", lambda e, ki=ki, wb=wb, rhs=rhs: e.matmul(
                            pe.ap(), wb.t[:, ki * 128:(ki + 1) * 128], rhs,
                            start=(ki == 0), stop=(ki == 7)),
                            reads=[wb, xqT], writes=[pe], inc=(ki == 7))
                    srep = _strided(st1rep.ap(), 1, [[2047, 2]])
                    k.op("v", lambda e, ct=ct, srep=srep: e.tensor_tensor(
                        edge.t[:, 2 * ct:2 * ct + 2], pe.ap(), srep, ALU.mult),
                        reads=[pe, st1rep], writes=[edge])
                ev = edge.t.rearrange("p (ct w) -> p ct w", w=2)
                sv = sedge.t.rearrange("p (ct w) -> p ct w", w=2)
                k.op("v", lambda e: e.memset(sv[:, :, 0], 0.0), writes=[sedge])
                k.op("v", lambda e: e.tensor_scalar(sv[:, :, 1], ev[:, :, 0],
                                                    flg.t[:, 1:2], None, ALU.mult),
                     reads=[edge, flg], writes=[sedge])
                k.op("q", lambda e: e.dma_start(
                    out=_apfull(eh_in.ap(), 0, [[2, 128], [256, NCT], [1, 2]]),
                in_=sedge.ap()),
                    reads=[sedge], writes=[eh_in])
                k.op("v", lambda e: e.tensor_scalar(sv[:, :, 0], ev[:, :, 1],
                                                    flg.t[:, 0:1], None, ALU.mult),
                     reads=[edge, flg], writes=[sedge])
                k.op("v", lambda e: e.memset(sv[:, :, 1], 0.0), writes=[sedge])
                k.op("q", lambda e: e.dma_start(
                    out=_apfull(eh_in.ap(), NCT * 128 * 2, [[2, 128], [256, NCT], [1, 2]]),
                in_=sedge.ap()),
                    reads=[sedge], writes=[eh_in])
                k.op("cc", lambda e: e.collective_compute(
                    "ReduceScatter", ALU.add, replica_groups=RG,
                    ins=[eh_in.ap()], outs=[eh_out.ap()]),
                    reads=[eh_in], writes=[eh_out])
                k.op("gd", lambda e: e.dma_start(
                    out=hrecv.ap(),
                in_=_apfull(eh_out.ap(), 0, [[2, 128], [256, NCT], [1, 2]])),
                    reads=[eh_out], writes=[hrecv])
                k.join()

        with ExitStack() as ces:
            ph = [k.ps(ces, f"s{sub}_ph{i}", [128, 512]) for i in range(2)]
            ptb = [k.ps(ces, f"s{sub}_ptbc{i}", [128, 512], BF16) for i in range(2)]
            cvb = [k.sb(ces, f"s{sub}_cv{i}", [128, TOK], BF16) for i in range(2)]
            mxs = k.sb(ces, f"s{sub}_mxs", [128, 1], F32)
            k.op("v", lambda e: e.memset(mxcol.ap(), 0.0), writes=[mxcol])
            nchunks = [(0, 512), (512, 512), (1024, 512), (1536, 512), (2048, 4)]
            cw, cb = conv_cols["w"], conv_cols["b"]
            for ct in range(NCT):
                wb = w1b[ct % 2]
                hbuf = hb[ct]
                cbuf = cvb[ct % 2]
                k.op("q", lambda e, ct=ct, wb=wb: e.dma_start(
                    out=_strided(wb.ap(), 0, [[128, 8], [1, 128]]),
                    in_=_apfull(wq1c.ap(), ct * 128,
                                [[DH, 128], [128 * DH, 8], [1, 128]])),
                    reads=[wq1c], writes=[wb])
                for ci, (n0, nn) in enumerate(nchunks):
                    pe = ph[ci % 2]
                    for ki in range(8):
                        k.op("t", lambda e, ki=ki, n0=n0, nn=nn, wb=wb, pe=pe: e.matmul(
                            pe.t[:, 0:nn], wb.t[:, ki * 128:(ki + 1) * 128],
                            xqT.t[:, TH * ki + n0: TH * ki + n0 + nn],
                            start=(ki == 0), stop=(ki == 7)),
                            reads=[wb, xqT], writes=[pe], inc=(ki == 7))
                    k.op("v", lambda e, n0=n0, nn=nn, pe=pe, hbuf=hbuf: e.tensor_tensor(
                        hbuf.t[:, n0:n0 + nn], pe.t[:, 0:nn],
                        st1rep.t[:, n0:n0 + nn], ALU.mult),
                        reads=[pe, st1rep], writes=[hbuf])
                if not first:
                    k.op("v", lambda e, ct=ct, hbuf=hbuf: e.tensor_copy(
                        hbuf.t[:, 0:1], hrecv.t[:, 2 * ct:2 * ct + 1]),
                        reads=[hrecv], writes=[hbuf])
                    k.op("v", lambda e, ct=ct, hbuf=hbuf: e.tensor_copy(
                        hbuf.t[:, 2049:2050], hrecv.t[:, 2 * ct + 1:2 * ct + 2]),
                        reads=[hrecv], writes=[hbuf])
                k.op("v", lambda e, ct=ct, hbuf=hbuf, cbuf=cbuf: e.tensor_scalar(
                    cbuf.ap(), hbuf.t[:, 1:2049], cw.t[:, 3 * ct + 1:3 * ct + 2],
                    cb.t[:, ct:ct + 1], ALU.mult, ALU.add),
                    reads=[hbuf, cw, cb], writes=[cbuf])
                k.op("v", lambda e, ct=ct, hbuf=hbuf, cbuf=cbuf: e.scalar_tensor_tensor(
                    cbuf.ap(), hbuf.t[:, 0:2048], cw.t[:, 3 * ct:3 * ct + 1], cbuf.ap(),
                    ALU.mult, ALU.add),
                    reads=[hbuf, cw, cbuf], writes=[cbuf])
                k.op("v", lambda e, ct=ct, hbuf=hbuf, cbuf=cbuf: e.scalar_tensor_tensor(
                    cbuf.ap(), hbuf.t[:, 2:2050], cw.t[:, 3 * ct + 2:3 * ct + 3], cbuf.ap(),
                    ALU.mult, ALU.add),
                    reads=[hbuf, cw, cbuf], writes=[cbuf])
                k.op("s", lambda e, hbuf=hbuf, cbuf=cbuf: e.activation(
                    hbuf.t[:, 1:2049], cbuf.ap(), AF.Silu),
                    reads=[cbuf], writes=[hbuf])
                if ct % 4 == 3:
                    for tb in range(NT):
                        pb = ptb[tb % 2]
                        for q in range(4):
                            k.op("t", lambda e, q=q, tb=tb, pb=pb: e.transpose(
                                pb.t[:, q * 128:(q + 1) * 128],
                                hb[ct - 3 + q].t[:, 1 + tb * 128:1 + (tb + 1) * 128],
                                g["eye"].ap()),
                                reads=[hb[ct - 3 + q], g["eye"]], writes=[pb])
                        k.op("v", lambda e, tb=tb, pb=pb: e.reduce_max(
                            mxs.ap(), pb.ap(), axis=AX.X,
                            apply_absolute_value=True),
                            reads=[pb], writes=[mxs])
                        k.op("v", lambda e, tb=tb: e.tensor_max(
                            mxcol.t[:, tb:tb + 1], mxcol.t[:, tb:tb + 1],
                            mxs.ap()),
                            reads=[mxs, mxcol], writes=[mxcol])
            k.join()

        with ExitStack() as des:
            prow = k.ps(des, f"s{sub}_prow2", [128, 128])
            k.op("v", lambda e: e.tensor_scalar(st2col.ap(), mxcol.ap(), 1e-5, None,
                                                ALU.max),
                 reads=[mxcol], writes=[st2col])
            k.op("v", lambda e: e.scalar_tensor_tensor(
                st2i05.ap(), st2col.t[:, 0:NT], 0.5 / 127.0,
                _strided(g["swinv"].ap(), 1, [[0, NT]]), ALU.mult, ALU.mult),
                reads=[st2col, g["swinv"]], writes=[st2i05])
            k.op("v", lambda e: e.reciprocal(st2col.ap(), st2col.ap()),
                 reads=[st2col], writes=[st2col])
            k.op("v", lambda e: e.tensor_scalar(st2col.ap(), st2col.ap(), 127.0, None,
                                                ALU.mult),
                 reads=[st2col], writes=[st2col])
            pst2 = k.ps(des, f"s{sub}_pst2", [32, 128])
            k.op("t", lambda e: e.transpose(pst2.ap(), st2col.ap(), g["eyef"].ap()),
                 reads=[st2col, g["eyef"]], writes=[pst2])
            k.op("v", lambda e: e.tensor_copy(strow.ap(), pst2.ap()),
                 reads=[pst2], writes=[strow])
            k.op("gd", lambda e: e.dma_start(
                out=_strided(strowf.ap(), 0, [[128, 18], [1, 128]]),
                in_=strow.t[0:18, :]),
                reads=[strow], writes=[strowf])
            for tt in range(NT):
                k.op("t", lambda e, tt=tt: e.matmul(prow.ap(), g["ones_row_b16"].t[0:1, 0:128],
                                                    strowf.t[:, tt * 128:(tt + 1) * 128],
                                                    start=True, stop=True),
                     reads=[g["ones_row_b16"], strowf], writes=[prow])
                k.op("v", lambda e, tt=tt: e.tensor_copy(
                    st2rep.t[:, 1 + tt * 128:1 + (tt + 1) * 128], prow.ap()),
                    reads=[prow], writes=[st2rep])
            k.join()

        with ExitStack() as ees:
            pm = [k.ps(ees, f"s{sub}_pm{i}", [128, 512]) for i in range(8)]
            w2b = [k.sb(ees, f"s{sub}_w2b{i}", [128, C], BF16) for i in range(4)]
            yt = [k.sb(ees, f"s{sub}_yt{i}", [128, C], F32) for i in range(2)]
            xrt = [k.sb(ees, f"s{sub}_xrt0", [128, C], F32)] * 2
            q2y = [k.sb(ees, f"s{sub}_q2y{i}", [128, 512], F32) for i in range(2)]
            q2z = [k.sb(ees, f"s{sub}_q2z{i}", [128, 512], I16) for i in range(2)]
            for tg in range(4):
                c0 = 1 + 512 * tg
                for ct in range(NCT):
                    hbuf = hb[ct]
                    yb, zb = q2y[ct % 2], q2z[ct % 2]
                    k.op("v", lambda e, hbuf=hbuf, yb=yb, c0=c0: e.tensor_tensor(
                        yb.ap(), hbuf.t[:, c0:c0 + 512], st2rep.t[:, c0:c0 + 512],
                        ALU.mult),
                        reads=[hbuf, st2rep], writes=[yb])
                    k.op("s", lambda e, yb=yb, zb=zb: e.activation(
                        zb.ap(), yb.ap(), AF.Copy, bias=16384.0),
                        reads=[yb], writes=[zb])
                    k.op("v", lambda e, yb=yb, zb=zb: e.scalar_tensor_tensor(
                        yb.ap(), zb.ap(), 0.5 - 16384.0, yb.ap(), ALU.add, ALU.is_le),
                        reads=[zb, yb], writes=[yb])
                    k.op("v", lambda e, yb=yb, zb=zb, hbuf=hbuf, c0=c0:
                         e.scalar_tensor_tensor(
                             hbuf.t[:, c0:c0 + 512], yb.ap(), -16384.0, zb.ap(),
                             ALU.add, ALU.add),
                         reads=[yb, zb], writes=[hbuf])
                    wb = w2b[ct % 4]
                    k.op("q", lambda e, ct=ct, wb=wb: e.dma_start(
                        out=wb.ap(), in_=wq2c.t[ct]),
                        reads=[wq2c], writes=[wb])
                    for ti in range(4):
                        tt = tg * 4 + ti
                        for dch in range(2):
                            pmx = pm[ti * 2 + dch]
                            last_of_ct = (ti == 3 and dch == 1)
                            stop = (ct == NCT - 1)
                            k.op("t", lambda e, ct=ct, tt=tt, dch=dch, wb=wb,
                                 pmx=pmx: e.matmul(
                                     pmx.ap(),
                                     hb[ct].t[:, 1 + tt * 128:1 + (tt + 1) * 128],
                                     wb.t[:, dch * 512:(dch + 1) * 512],
                                     start=(ct == 0), stop=stop),
                                 reads=[hb[ct], wb],
                                 writes=[pmx] if stop else [],
                                 inc=(stop or last_of_ct))
                for ti in range(4):
                    tt = tg * 4 + ti
                    ybuf, xrb = yt[ti % 2], xrt[ti % 2]
                    src = xsrc
                    k.op("q", lambda e, tt=tt, xrb=xrb, src=src: e.dma_start(
                        out=xrb.ap(), in_=src.t[tt * 128:(tt + 1) * 128, :]),
                        reads=[src], writes=[xrb])
                    for dch in range(2):
                        pmx = pm[ti * 2 + dch]
                        k.op("s", lambda e, tt=tt, dch=dch, pmx=pmx, ybuf=ybuf:
                             e.activation(ybuf.t[:, dch * 512:(dch + 1) * 512],
                                          pmx.ap(), AF.Copy,
                                          scale=st2i05.t[:, tt:tt + 1]),
                             reads=[pmx, st2i05], writes=[ybuf])
                    k.op("v", lambda e, ybuf=ybuf, xrb=xrb: e.tensor_add(
                        ybuf.ap(), ybuf.ap(), xrb.ap()),
                        reads=[ybuf, xrb], writes=[ybuf])
                    k.op("o", lambda e, tt=tt, ybuf=ybuf: e.dma_start(
                        out=xdst.t[tt * 128:(tt + 1) * 128, :], in_=ybuf.ap()),
                        reads=[ybuf], writes=[xdst])
            k.join()


def _ln_transpose(k, es, g, sub, xsrc, xlnT, sink=None, dump=None,
                  stage_buf=None):
    NDEP = 4
    with ExitStack() as aes:
        if xlnT is not None:
            stage = None
        elif stage_buf is not None:
            stage = [(stage_buf, 0)] + [(Buf(stage_buf.t), 1024 * j)
                                        for j in range(1, NDEP)]
        else:
            sb0 = k.sb(aes, f"s{sub}_lstg0", [128, NDEP * 8 * 128], BF16)
            stage = [(sb0, 0)] + [(Buf(sb0.t), 1024 * j) for j in range(1, NDEP)]
        ptb = [k.ps(aes, f"s{sub}_lptb{i}", [128, 128], BF16) for i in range(2)]
        xt = [k.sb(aes, f"s{sub}_lxt{i}", [128, C], F32) for i in range(NDEP)]
        xlnb = [k.sb(aes, f"s{sub}_lxln{i}", [128, C], BF16) for i in range(NDEP)]
        sdump = dump if dump is not None else k.sb(aes, f"s{sub}_lsd", [128, C], BF16)
        colsb = [k.sb(aes, f"s{sub}_lcols{i}", [128, 6], F32) for i in range(NDEP)]
        for tt in range(NT):
            i = tt % NDEP
            xb, cols, xl = xt[i], colsb[i], xlnb[i]
            k.op("q", lambda e, tt=tt, xb=xb: e.dma_start(
                out=xb.ap(), in_=xsrc.t[tt * 128:(tt + 1) * 128, :]),
                reads=[xsrc], writes=[xb])
            k.op("v", lambda e, xb=xb, cols=cols: e.reduce_sum(
                cols.t[:, 0:1], xb.ap(), axis=AX.X), reads=[xb], writes=[cols])
            k.op("s", lambda e, xb=xb, cols=cols: e.activation(
                sdump.t[:, 0:C], xb.ap(), AF.Square, accum_out=cols.t[:, 1:2]),
                reads=[xb], writes=[sdump, cols])
            k.op("v", lambda e, cols=cols: e.tensor_scalar(
                cols.t[:, 2:3], cols.t[:, 0:1], 1.0 / C, None, ALU.mult),
                reads=[cols], writes=[cols])
            k.op("v", lambda e, cols=cols: e.tensor_mul(
                cols.t[:, 3:4], cols.t[:, 2:3], cols.t[:, 2:3]),
                reads=[cols], writes=[cols])
            k.op("v", lambda e, cols=cols: e.scalar_tensor_tensor(
                cols.t[:, 3:4], cols.t[:, 1:2], 1.0 / C, cols.t[:, 3:4],
                ALU.mult, ALU.subtract), reads=[cols], writes=[cols])
            k.op("s", lambda e, cols=cols: e.activation(
                cols.t[:, 4:5], cols.t[:, 3:4], AF.Sqrt, bias=g["eps"].ap()),
                reads=[cols, g["eps"]], writes=[cols])
            k.op("v", lambda e, cols=cols: e.reciprocal(
                cols.t[:, 4:5], cols.t[:, 4:5]), reads=[cols], writes=[cols])
            k.op("v", lambda e, cols=cols: e.scalar_tensor_tensor(
                cols.t[:, 5:6], cols.t[:, 2:3], -1.0, cols.t[:, 4:5],
                ALU.mult, ALU.mult), reads=[cols], writes=[cols])
            k.op("s", lambda e, xb=xb, cols=cols, xl=xl: e.activation(
                xl.ap(), xb.ap(), AF.Identity, bias=cols.t[:, 5:6],
                scale=cols.t[:, 4:5]), reads=[xb, cols], writes=[xl])
            stg, soff = stage[tt % NDEP] if stage is not None else (None, 0)
            for j in range(8):
                pb = ptb[j % 2]
                k.op("t", lambda e, j=j, xl=xl, pb=pb: e.transpose(
                    pb.ap(), xl.t[:, j * 128:(j + 1) * 128], g["eye"].ap()),
                    reads=[xl, g["eye"]], writes=[pb])
                if xlnT is not None:
                    k.op("v", lambda e, j=j, tt=tt, pb=pb: e.tensor_copy(
                        xlnT.t[:, TOK * j + tt * 128: TOK * j + (tt + 1) * 128],
                        pb.ap()), reads=[pb], writes=[xlnT])
                else:
                    k.op("v", lambda e, j=j, pb=pb, stg=stg, soff=soff: e.tensor_copy(
                        stg.t[:, soff + 128 * j:soff + 128 * (j + 1)], pb.ap()),
                        reads=[pb], writes=[stg])
            if stage is not None:
                sink(tt, stg, soff)
        k.join()


def _moe(k, g, sub, xsrc, xdst, routerw, expertw):
    with ExitStack() as es:
        xlnT = k.sb(es, f"s{sub}_xlnT", [128, 8 * TOK], BF16)
        _ln_transpose(k, es, g, sub, xsrc, xlnT)
        rw = k.sb(es, f"s{sub}_rw", [128, 9 * 8], BF16)
        yacc = [k.sb(es, f"s{sub}_y{tt}", [128, C], F32) for tt in range(NT)]
        cgate = k.sb(es, f"s{sub}_cg", [128, NT * 8], F32)
        ex = k.sb(es, f"s{sub}_ex", [128, 16], F32)
        gcol = k.sb(es, f"s{sub}_gcol", [128, 2], F32)
        xrt = [k.sb(es, f"s{sub}_xr{i}", [128, C], F32) for i in range(2)]

        k.op("gd", lambda e: e.dma_start(
            out=_strided(rw.ap(), 0, [[8, 8], [1, 8]]),
            in_=_apfull(routerw.ap(), 0, [[8, 128], [1024, 8], [1, 8]])),
            reads=[routerw], writes=[rw])
        k.op("gd", lambda e: e.dma_start(
            out=rw.t[0:1, 64:72], in_=routerw.t[1024:1025, :]),
            reads=[routerw], writes=[rw])

        with ExitStack() as bes:
            plg = k.ps(bes, f"s{sub}_plg", [128, 8])
            for tt in range(NT):
                for ki in range(8):
                    k.op("t", lambda e, tt=tt, ki=ki: e.matmul(
                        plg.ap(), xlnT.t[:, TOK * ki + tt * 128: TOK * ki + (tt + 1) * 128],
                        rw.t[:, 8 * ki:8 * ki + 8], start=(ki == 0), stop=False),
                        reads=[xlnT, rw], writes=[plg], inc=False)
                k.op("t", lambda e, tt=tt: e.matmul(
                    plg.ap(), g["ones_row_b16"].t[0:1, 0:128], rw.t[0:1, 64:72],
                    start=False, stop=True),
                    reads=[g["ones_row_b16"], rw], writes=[plg])
                k.op("v", lambda e: e.reduce_max(gcol.t[:, 0:1], plg.ap(), axis=AX.X),
                     reads=[plg], writes=[gcol])
                k.op("v", lambda e: e.tensor_scalar(gcol.t[:, 0:1], gcol.t[:, 0:1],
                                                    -1.0, None, ALU.mult),
                     reads=[gcol], writes=[gcol])
                k.op("s", lambda e: e.activation(ex.t[:, 0:8], plg.ap(), AF.Exp,
                                                 bias=gcol.t[:, 0:1]),
                     reads=[plg, gcol], writes=[ex])
                k.op("v", lambda e: e.reduce_sum(gcol.t[:, 1:2], ex.t[:, 0:8], axis=AX.X),
                     reads=[ex], writes=[gcol])
                k.op("v", lambda e: e.reciprocal(gcol.t[:, 1:2], gcol.t[:, 1:2]),
                     reads=[gcol], writes=[gcol])
                k.op("v", lambda e: e.tensor_scalar(ex.t[:, 8:16], ex.t[:, 0:8],
                                                    1.0, None, ALU.is_equal),
                     reads=[ex], writes=[ex])
                k.op("v", lambda e, tt=tt: e.scalar_tensor_tensor(
                    cgate.t[:, 8 * tt:8 * tt + 8], ex.t[:, 8:16], 0.5,
                    _strided(gcol.ap(), 1, [[0, 8]]), ALU.mult, ALU.mult),
                    reads=[ex, gcol], writes=[cgate])
            k.join()

        _moe_experts(k, es, g, sub, xlnT, expertw, cgate, yacc, xsrc, xdst, xrt)


def _moe_experts(k, es, g, sub, xlnT, expertw, cgate, yacc, xsrc, xdst, xrt):
    with ExitStack() as ces:
        pm = [k.ps(ces, f"s{sub}_epm{i}", [128, 512]) for i in range(8)]
        web = [k.sb(ces, f"s{sub}_ewe{i}", [128, C], BF16) for i in range(9)]
        webr = k.sb(ces, f"s{sub}_ewebr", [1, C], BF16)
        for e_ in range(NE):
            k.op("gd", lambda e, e_=e_: e.dma_start(
                out=webr.ap(), in_=expertw.t[e_, 1024:1025, :]),
                reads=[expertw], writes=[webr])
            for ki in range(8):
                wb = web[(e_ + ki) % 9]
                k.op("gd", lambda e, e_=e_, ki=ki, wb=wb: e.dma_start(
                    out=wb.ap(), in_=expertw.t[e_, ki * 128:(ki + 1) * 128, :]),
                    reads=[expertw], writes=[wb])
            for tg in range(4):
                for ki in range(8):
                    wb = web[(e_ + ki) % 9]
                    for ti in range(4):
                        tt = tg * 4 + ti
                        for dch in range(2):
                            pmx = pm[2 * ti + dch]
                            last_of_ki = (ti == 3 and dch == 1)
                            k.op("t", lambda e, tt=tt, ki=ki, dch=dch, wb=wb,
                                 pmx=pmx: e.matmul(
                                     pmx.ap(),
                                     xlnT.t[:, TOK * ki + tt * 128: TOK * ki + (tt + 1) * 128],
                                     wb.t[:, dch * 512:(dch + 1) * 512],
                                     start=(ki == 0), stop=False),
                                 reads=[xlnT, wb],
                                 writes=[pmx] if ki == 0 else [],
                                 inc=last_of_ki)
                for ti in range(4):
                    tt = tg * 4 + ti
                    for dch in range(2):
                        pmx = pm[2 * ti + dch]
                        k.op("t", lambda e, tt=tt, dch=dch, pmx=pmx: e.matmul(
                            pmx.ap(), g["ones_row_b16"].t[0:1, 0:128],
                            webr.t[0:1, dch * 512:(dch + 1) * 512],
                            start=False, stop=True),
                            reads=[g["ones_row_b16"], webr], writes=[pmx])
                        ya = yacc[tt]
                        if e_ == 0:
                            k.op("v", lambda e, tt=tt, e_=e_, dch=dch, pmx=pmx,
                                 ya=ya: e.tensor_scalar(
                                     ya.t[:, dch * 512:(dch + 1) * 512], pmx.ap(),
                                     cgate.t[:, 8 * tt + e_:8 * tt + e_ + 1],
                                     None, ALU.mult),
                                 reads=[pmx, cgate], writes=[ya])
                        else:
                            k.op("v", lambda e, tt=tt, e_=e_, dch=dch, pmx=pmx,
                                 ya=ya: e.scalar_tensor_tensor(
                                     ya.t[:, dch * 512:(dch + 1) * 512], pmx.ap(),
                                     cgate.t[:, 8 * tt + e_:8 * tt + e_ + 1],
                                     ya.t[:, dch * 512:(dch + 1) * 512],
                                     ALU.mult, ALU.add),
                                 reads=[pmx, cgate, ya], writes=[ya])
        for tt in range(NT):
            xrb = xrt[tt % 2]
            k.op("q", lambda e, tt=tt, xrb=xrb: e.dma_start(
                out=xrb.ap(), in_=xsrc.t[tt * 128:(tt + 1) * 128, :]),
                reads=[xsrc], writes=[xrb])
            k.op("v", lambda e, tt=tt, xrb=xrb: e.tensor_add(
                yacc[tt].ap(), yacc[tt].ap(), xrb.ap()),
                reads=[yacc[tt], xrb], writes=[yacc[tt]])
            k.op("o", lambda e, tt=tt: e.dma_start(
                out=xdst.t[tt * 128:(tt + 1) * 128, :], in_=yacc[tt].ap()),
                reads=[yacc[tt]], writes=[xdst])
        k.join()


def _attention(k, g, sub, layer, xsrc, xdst, qkvw, projw, slopes8,
               ag_in, ag_out, rs_in, rs_out):
    NQB = 8
    with ExitStack() as es:
        attnP = k.sb(es, f"s{sub}_attnP", [128, (NH // 2) * NQB * 512], BF16)
        s123 = es.enter_context(ExitStack())
        xlnTF = k.sb(s123, f"s{sub}_xlnTF", [128, 8 * N], BF16)
        qbuf = [k.sb(s123, f"s{sub}_qb{i}", [70, N], BF16) for i in range(2)]
        kbuf = [k.sb(s123, f"s{sub}_kb{i}", [70, N], BF16) for i in range(2)]
        if layer == 0:
            cost = k.sb(s123, f"s{sub}_cost", [128, 2048], BF16)
            sint = k.sb(s123, f"s{sub}_sint", [128, 2048], BF16)
        else:
            cost = sint = None
        vbuf = [k.sb(s123, f"s{sub}_vb{i}", [128, 32 * 130], BF16) for i in range(2)]
        wqkv = k.sb(s123, f"s{sub}_wqkv", [128, 8 * 384], BF16)
        biasq = k.sb(s123, f"s{sub}_biasq", [128, 64], F32)
        ccol = k.sb(s123, f"s{sub}_ccol", [128, 8], F32)
        bcol2 = k.sb(s123, f"s{sub}_bcol2", [128, 64], F32)
        kposc = k.sb(s123, f"s{sub}_kposc", [128, 32], F32)
        dtile = [k.sb(s123, f"s{sub}_dt{j}", [128, 512], F16) for j in range(4)]
        coefr = k.sb(s123, f"s{sub}_coefr", [48, 512], BF16)
        rrow = k.sb(s123, f"s{sub}_rrow", [1, 512], F32)
        nrm = {
            "cols": [k.sb(s123, f"s{sub}_ncol{i}", [128, 64], F32) for i in range(2)],
            "dump": k.sb(s123, f"s{sub}_ndump", [128, 128], BF16),
            "rowq": k.sb(s123, f"s{sub}_nrq", [1, 16], F32),
            "rowk": k.sb(s123, f"s{sub}_nrk", [1, 16], F32),
            "work": k.sb(s123, f"s{sub}_nwork", [1, 16], F32),
        }
        mC = k.sb(s123, f"s{sub}_mC", [128, 16], F32)

        with ExitStack() as aes:
            def sink(tt, stg, soff):
                k.op("q", lambda e, tt=tt, stg=stg, soff=soff: e.dma_start(
                    out=_apfull(ag_in.ap(), 128 * tt,
                                [[TOK, 128], [128 * TOK, 8], [1, 128]]),
                    in_=_strided(stg.ap(), soff, [[128, 8], [1, 128]])),
                    reads=[stg], writes=[ag_in])
            _ln_transpose(k, aes, g, sub, xsrc, None, sink=sink,
                          dump=vbuf[1], stage_buf=vbuf[0])
            k.op("cc", lambda e: e.collective_compute(
                "AllGather", ALU.bypass, replica_groups=RG,
                ins=[ag_in.ap()], outs=[ag_out.ap()]),
                reads=[ag_in], writes=[ag_out])
            for r in range(2):
                k.op("q", lambda e, r=r: e.dma_start(
                    out=_strided(xlnTF.ap(), r * TOK, [[N, 8], [1, TOK]]),
                    in_=_apfull(ag_out.ap(), r * 8 * 128 * TOK,
                                [[TOK, 128], [128 * TOK, 8], [1, TOK]])),
                    reads=[ag_out], writes=[xlnTF])
            k.join()

        k.op("g", lambda e: e.iota(kposc.ap(), pattern=[[128, 32]], base=0,
                                   channel_multiplier=1,
                                   allow_small_or_imprecise_dtypes=True),
             writes=[kposc])
        if layer == 0:
            k.op("g", lambda e: e.iota(
                attnP.t[0:1, 0:512], pattern=[[1, 32], [0, 16]],
                base=0, channel_multiplier=0,
                allow_small_or_imprecise_dtypes=True), writes=[attnP])
            k.op("g", lambda e: e.iota(
                attnP.t[0:1, 8192:8704], pattern=[[0, 32], [1, 16]],
                base=0, channel_multiplier=0,
                allow_small_or_imprecise_dtypes=True), writes=[attnP])
            for w in (512, 1024, 2048):
                k.op("v", lambda e, w=w: e.tensor_copy(
                    attnP.t[0:1, w:2 * w], attnP.t[0:1, 0:w]),
                    reads=[attnP], writes=[attnP])
                k.op("v", lambda e, w=w: e.tensor_copy(
                    attnP.t[0:1, 8192 + w:8192 + 2 * w],
                    attnP.t[0:1, 8192:8192 + w]),
                    reads=[attnP], writes=[attnP])
            for i in range(2):
                for r in range(6):
                    srco = 0 if r < 3 else 8192
                    k.op("gd", lambda e, i=i, r=r, srco=srco: e.dma_start(
                        out=qbuf[i].t[64 + r:65 + r, 0:4096],
                        in_=attnP.t[0:1, srco:srco + 4096]),
                        reads=[attnP], writes=[qbuf[i]])
            with ExitStack() as bes:
                pbc = k.ps(bes, f"s{sub}_pbc", [128, 8])
                slrow = k.sb(bes, f"s{sub}_slrow", [1, 32], F32)
                k.op("q", lambda e: e.dma_start(out=slrow.t[0:1, 0:8],
                                                in_=slopes8.ap()),
                     reads=[slopes8], writes=[slrow])
                pco = k.ps(bes, f"s{sub}_pco", [48, 512])
                jio = k.sb(bes, f"s{sub}_jio", [128, 512], F32)
                cval = k.sb(bes, f"s{sub}_cval", [1, 48], BF16)
                k.op("t", lambda e: e.matmul(pbc.ap(), g["ones_row_f"].ap(),
                                             slrow.t[0:1, 0:8], start=True, stop=True),
                     reads=[g["ones_row_f"], slrow], writes=[pbc])
                k.op("v", lambda e: e.tensor_scalar(ccol.ap(), pbc.ap(), 8.0, None,
                                                    ALU.mult), reads=[pbc], writes=[ccol])
                for h in range(NH):
                    for qb in range(NQB):
                        k.op("v", lambda e, h=h, qb=qb: e.tensor_scalar(
                            bcol2.t[:, 8 * h + qb:8 * h + qb + 1],
                            ccol.t[:, h:h + 1], float(-512 * qb), -26.0,
                            ALU.mult, ALU.add), reads=[ccol], writes=[bcol2])
                k.op("v", lambda e: e.tensor_scalar(
                    _strided(slrow.ap(), 8, [[1, 8]]), slrow.t[0:1, 0:8],
                    -128.0, None, ALU.mult), reads=[slrow], writes=[slrow])
                k.op("v", lambda e: e.tensor_scalar(
                    _strided(slrow.ap(), 16, [[1, 8]]), slrow.t[0:1, 0:8],
                    -8.0, None, ALU.mult), reads=[slrow], writes=[slrow])
                for base, cv0 in ((8, 0), (16, 3)):
                    for lvl in range(3):
                        k.op("v", lambda e, base=base, cv0=cv0, lvl=lvl: e.tensor_copy(
                            _strided(cval.ap(), cv0 + lvl, [[6, 8]]),
                            _strided(slrow.ap(), base, [[1, 8]])),
                            reads=[slrow], writes=[cval])
                        if lvl < 2:
                            k.op("v", lambda e, base=base, cv0=cv0, lvl=lvl:
                                 e.tensor_sub(
                                     _strided(slrow.ap(), base, [[1, 8]]),
                                     _strided(slrow.ap(), base, [[1, 8]]),
                                     _strided(cval.ap(), cv0 + lvl, [[6, 8]])),
                                 reads=[slrow, cval], writes=[slrow])
                k.op("t", lambda e: e.matmul(pco.ap(), cval.ap(),
                                             g["ones_row_b16"].ap(),
                                             start=True, stop=True),
                     reads=[cval, g["ones_row_b16"]], writes=[pco])
                k.op("v", lambda e: e.tensor_copy(coefr.ap(), pco.ap()),
                     reads=[pco], writes=[coefr])
                k.op("g", lambda e: e.iota(jio.ap(), pattern=[[1, 512]], base=0,
                                           channel_multiplier=0,
                                           allow_small_or_imprecise_dtypes=True),
                     writes=[jio])
                for jj in range(4):
                    k.op("g", lambda e, jj=jj: e.iota(
                        dtile[jj].ap(), pattern=[[0, 512]], base=128 * jj,
                        channel_multiplier=1,
                        allow_small_or_imprecise_dtypes=True), writes=[dtile[jj]])
                    k.op("v", lambda e, jj=jj: e.tensor_tensor(
                        dtile[jj].ap(), dtile[jj].ap(), jio.ap(), ALU.min),
                        reads=[dtile[jj], jio], writes=[dtile[jj]])
                k.join()
            _build_rope_tables(k, es, g, sub, cost, sint)
        else:
            with ExitStack() as bes:
                jio = k.sb(bes, f"s{sub}_jio", [128, 512], F32)
                k.op("g", lambda e: e.iota(jio.ap(), pattern=[[1, 512]], base=0,
                                           channel_multiplier=0,
                                           allow_small_or_imprecise_dtypes=True),
                     writes=[jio])
                for jj in range(4):
                    k.op("g", lambda e, jj=jj: e.iota(
                        dtile[jj].ap(), pattern=[[0, 512]], base=128 * jj,
                        channel_multiplier=1,
                        allow_small_or_imprecise_dtypes=True), writes=[dtile[jj]])
                    k.op("v", lambda e, jj=jj: e.tensor_tensor(
                        dtile[jj].ap(), dtile[jj].ap(), jio.ap(), ALU.is_le),
                        reads=[dtile[jj], jio], writes=[dtile[jj]])
                    k.op("v", lambda e, jj=jj: e.tensor_scalar(
                        dtile[jj].ap(), dtile[jj].ap(), -1.0, 1e30,
                        ALU.add, ALU.mult), reads=[dtile[jj]], writes=[dtile[jj]])
                k.join()
        k.join()

        with ExitStack() as ces:
            pq = [k.ps(ces, f"s{sub}_pq{i}", [128, 512]) for i in range(2)]
            ptr_t = k.ps(ces, f"s{sub}_ptr", [128, 256], BF16)
            ptr = [ptr_t, Buf(ptr_t.t)]
            pe2 = [k.ps(ces, f"s{sub}_pe{i}", [128, 512]) for i in range(2)]
            pe = [pq[0], pq[1], pe2[0], pe2[1]]
            pv = [k.ps(ces, f"s{sub}_pv{i}", [65, 512]) for i in range(2)]
            prep = k.ps(ces, f"s{sub}_prep", [65, 512])
            ropes = [k.sb(ces, f"s{sub}_ropes_{j}", [128, 128], F32)
                     for j in range(2)]
            ropes2 = [k.sb(ces, f"s{sub}_ropes2_{j}", [128, 128], F32)
                      for j in range(2)]
            ropesb = [k.sb(ces, f"s{sub}_ropesb_{j}", [128, 128], BF16)
                      for j in range(2)]
            preps = k.sb(ces, f"s{sub}_preps", [65, 512], F32)
            nrm["col2"] = k.sb(ces, f"s{sub}_ncol2", [128, 2], F32)
            nrm["pn"] = pe2[0]
            nrm["pn2"] = pe2[1]
            nrm["pmc"] = pq[0]
            wT = [k.sb(ces, f"s{sub}_wT{i}", [128, 512], BF16) for i in range(4)]

            for vb in vbuf:
                k.op("v", lambda e, vb=vb: e.memset(
                    _strided(vb.ap(), 64, [[65, 64]]), 1.0), writes=[vb])

            for hp in range(4):
                wq_b = wqkv
                vb_b = vbuf[hp % 2]
                for part in range(3):
                    k.op("gd", lambda e, hp=hp, part=part, wq_b=wq_b: e.dma_start(
                        out=_strided(wq_b.ap(), 128 * part, [[384, 8], [1, 128]]),
                        in_=_apfull(qkvw.ap(), 512 * part + 128 * hp,
                                    [[1536, 128], [128 * 1536, 8], [1, 128]])),
                        reads=[qkvw], writes=[wq_b])
                for ki in range(8):
                    k.op("v", lambda e, ki=ki, wq_b=wq_b: e.tensor_scalar(
                        wq_b.t[:, 384 * ki:384 * ki + 128],
                        wq_b.t[:, 384 * ki:384 * ki + 128], 0.125, None, ALU.mult),
                        reads=[wq_b], writes=[wq_b])
                if layer == 0:
                    for i, h in enumerate((2 * hp, 2 * hp + 1)):
                        kb = kbuf[i]
                        for nc_ in range(8):
                            k.op("q", lambda e, h=h, kb=kb, nc_=nc_: e.dma_start(
                                out=kb.t[64:70, 512 * nc_:512 * (nc_ + 1)],
                                in_=coefr.t[6 * h:6 * h + 6, :]),
                                reads=[coefr], writes=[kb])
                _qkv_rope(k, g, sub, hp, wq_b, xlnTF, qbuf, kbuf,
                          vb_b, pq, ptr, ropes, ropes2, ropesb,
                          cost, sint, nrm, use_rope=(layer == 0))
                _shift_consts(k, g, sub, layer, hp, nrm, mC)
                for i in range(2):
                    h = 2 * hp + i
                    if layer == 0:
                        for qb in range(8):
                            k.op("v", lambda e, h=h, i=i, qb=qb:
                                 e.scalar_tensor_tensor(
                                     bcol2.t[:, 8 * h + qb:8 * h + qb + 1],
                                     ccol.t[:, h:h + 1], float(-512 * qb),
                                     mC.t[:, 8 * i + qb:8 * i + qb + 1],
                                     ALU.mult, ALU.add),
                                 reads=[ccol, mC], writes=[bcol2])

                    def q_ap(KK, qb, i=i):
                        return qbuf[i].t[0:KK, 512 * qb:512 * (qb + 1)]

                    def k_ap(KK, kc, i=i):
                        return kbuf[i].t[0:KK, 128 * kc:128 * (kc + 1)]
                    qsrc, ksrc = qbuf[i], kbuf[i]
                    _attn_head(k, g, sub, layer, h, i, q_ap, k_ap, qsrc, ksrc,
                               vb_b, attnP, pe, pv, prep, preps, wT, ccol, bcol2,
                               kposc, biasq, dtile, mC, rrow)
            k.join()

        k.join()
        s123.close()
        with ExitStack() as des:
            pp = [k.ps(des, f"s{sub}_pp{i}", [128, 512]) for i in range(4)]
            pst = [k.sb(des, f"s{sub}_pst{i}", [128, C], F32) for i in range(2)]
            xrt2 = [k.sb(des, f"s{sub}_xrt{i}", [128, C], F32) for i in range(2)]
            rst = [k.sb(des, f"s{sub}_rst{i}", [128, C], F32) for i in range(2)]
            pwh = k.sb(des, f"s{sub}_pwh", [128, (NH // 2) * C], BF16)
            pwb = k.sb(des, f"s{sub}_pwb", [1, C], BF16)
            k.op("gd", lambda e: e.dma_start(
                out=pwh.t[0:64, :],
                in_=_apfull(projw.ap(), 0, [[C, 64], [128 * C, NH // 2], [1, C]])),
                reads=[projw], writes=[pwh])
            k.op("gd", lambda e: e.dma_start(
                out=pwh.t[64:128, :],
                in_=_apfull(projw.ap(), 64 * C, [[C, 64], [128 * C, NH // 2], [1, C]])),
                reads=[projw], writes=[pwh])
            k.op("gd", lambda e: e.dma_start(
                out=pwb.ap(), in_=projw.t[512:513, :]), reads=[projw], writes=[pwb])
            rs_in_c = [rs_in] + [Buf(rs_in.t) for _ in range(3)]
            rs_out_c = [rs_out] + [Buf(rs_out.t) for _ in range(3)]
            for cch in range(4):
                for half in range(2):
                    for t4 in range(4):
                        ts = 16 * half + 4 * cch + t4
                        qb, tsl = ts // 4, ts % 4
                        st = pst[ts % 2]
                        for dch in range(2):
                            pmx = pp[2 * (ts % 2) + dch]
                            for h in range(NH // 2):
                                k.op("t", lambda e, h=h, qb=qb, tsl=tsl, dch=dch,
                                     pmx=pmx: e.matmul(
                                         pmx.ap(),
                                         attnP.t[:, 4096 * h + 512 * qb + 128 * tsl:
                                                 4096 * h + 512 * qb + 128 * (tsl + 1)],
                                         pwh.t[:, C * h + 512 * dch:C * h + 512 * (dch + 1)],
                                         start=(h == 0), stop=False),
                                     reads=[attnP, pwh],
                                     writes=[pmx] if h == 0 else [], inc=False)
                            k.op("t", lambda e, dch=dch, pmx=pmx: e.matmul(
                                pmx.ap(), g["ones_row_b16"].t[0:1, 0:128],
                                pwb.t[0:1, dch * 512:(dch + 1) * 512],
                                start=False, stop=True),
                                reads=[g["ones_row_b16"], pwb], writes=[pmx])
                            k.op("v", lambda e, dch=dch, pmx=pmx, st=st: e.tensor_copy(
                                st.t[:, dch * 512:(dch + 1) * 512], pmx.ap()),
                                reads=[pmx], writes=[st])
                        k.op("q", lambda e, cch=cch, half=half, t4=t4, st=st:
                             e.dma_start(
                                 out=rs_in.t[cch, half, t4 * 128:(t4 + 1) * 128, :],
                                 in_=st.ap()),
                             reads=[st], writes=[rs_in_c[cch]])
                k.op("cc", lambda e, cch=cch: e.collective_compute(
                    "ReduceScatter", ALU.add, replica_groups=RG,
                    ins=[rs_in.t[cch]], outs=[rs_out.t[cch]]),
                    reads=[rs_in_c[cch]], writes=[rs_out_c[cch]])
            for cch in range(4):
                for t4 in range(4):
                    tt = 4 * cch + t4
                    rb = rst[tt % 2]
                    xrb = xrt2[tt % 2]
                    k.op("gd", lambda e, cch=cch, t4=t4, rb=rb: e.dma_start(
                        out=rb.ap(), in_=rs_out.t[cch, t4 * 128:(t4 + 1) * 128, :]),
                        reads=[rs_out_c[cch]], writes=[rb])
                    k.op("gd", lambda e, tt=tt, xrb=xrb: e.dma_start(
                        out=xrb.ap(), in_=xsrc.t[tt * 128:(tt + 1) * 128, :]),
                        reads=[xsrc], writes=[xrb])
                    k.op("v", lambda e, rb=rb, xrb=xrb: e.tensor_add(
                        rb.ap(), rb.ap(), xrb.ap()),
                        reads=[rb, xrb], writes=[rb])
                    k.op("o", lambda e, tt=tt, rb=rb: e.dma_start(
                        out=xdst.t[tt * 128:(tt + 1) * 128, :], in_=rb.ap()),
                        reads=[rb], writes=[xdst])
            k.join()


def _shift_consts(k, g, sub, layer, hp, nrm, mC):
    rowq, rowk, work = nrm["rowq"], nrm["rowk"], nrm["work"]
    for i in range(2):
        if layer == 1:
            for qb in range(1, 8):
                k.op("v", lambda e, i=i, qb=qb: e.tensor_max(
                    rowk.t[0:1, 8 * i + qb:8 * i + qb + 1],
                    rowk.t[0:1, 8 * i + qb:8 * i + qb + 1],
                    rowk.t[0:1, 8 * i + qb - 1:8 * i + qb]),
                    reads=[rowk], writes=[rowk])
            k.op("v", lambda e, i=i: e.tensor_mul(
                work.t[0:1, 0:8], rowq.t[0:1, 8 * i:8 * i + 8],
                rowk.t[0:1, 8 * i:8 * i + 8]),
                reads=[rowq, rowk], writes=[work])
        else:
            k.op("v", lambda e, i=i: e.tensor_reduce(
                work.t[0:1, 8:9], rowk.t[0:1, 8 * i:8 * i + 8], AX.X, ALU.max),
                reads=[rowk], writes=[work])
            k.op("v", lambda e, i=i: e.tensor_scalar(
                work.t[0:1, 0:8], rowq.t[0:1, 8 * i:8 * i + 8],
                work.t[0:1, 8:9], None, ALU.mult),
                reads=[rowq, work], writes=[work])
        k.op("s", lambda e: e.activation(work.t[0:1, 0:8], work.t[0:1, 0:8],
                                         AF.Sqrt),
             reads=[work], writes=[work])
        k.op("v", lambda e: e.tensor_scalar(work.t[0:1, 0:8], work.t[0:1, 0:8],
                                            -1.0, 80.0, ALU.mult, ALU.add),
             reads=[work], writes=[work])
        k.op("v", lambda e: e.tensor_scalar(work.t[0:1, 0:8], work.t[0:1, 0:8],
                                            0.0, None, ALU.min),
             reads=[work], writes=[work])
        k.op("t", lambda e, i=i: e.matmul(
            nrm["pmc"].t[:, 0:8], g["ones_row_f"].ap(), work.t[0:1, 0:8],
            start=True, stop=True),
            reads=[g["ones_row_f"], work], writes=[nrm["pmc"]])
        k.op("v", lambda e, i=i: e.tensor_copy(mC.t[:, 8 * i:8 * i + 8],
                                               nrm["pmc"].t[:, 0:8]),
             reads=[nrm["pmc"]], writes=[mC])


def _qkv_rope(k, g, sub, hp, wqkv, xlnTF, qbuf, kbuf, vbuf, pq, ptr,
              ropes, ropes2, ropesb, cost, sint, nrm, use_rope):
    eye = g["eye"]
    ncol = nrm["cols"]
    for tt in range(32):
        pqx = pq[tt % 2]
        for ki in range(8):
            k.op("t", lambda e, tt=tt, ki=ki, pqx=pqx: e.matmul(
                pqx.t[:, 0:384],
                xlnTF.t[:, N * ki + 128 * tt: N * ki + 128 * (tt + 1)],
                wqkv.t[:, 384 * ki:384 * ki + 384],
                start=(ki == 0), stop=(ki == 7)),
                reads=[xlnTF, wqkv], writes=[pqx], inc=(ki == 7))
        for j, (off, dst2, ncb) in enumerate(((0, qbuf, 0), (128, kbuf, 32))):
            rs, rs2, rsb, pt = ropes[j], ropes2[j], ropesb[j], ptr[j]
            if use_rope:
                k.op("v", lambda e, pqx=pqx, tt=tt, off=off, rs=rs: e.tensor_tensor(
                    rs.ap(), pqx.t[:, off:off + 128],
                    _strided(cost.ap(), 64 * tt, [[0, 2], [1, 64]]), ALU.mult),
                    reads=[pqx, cost], writes=[rs])
                k.op("v", lambda e, pqx=pqx, tt=tt, off=off, rs2=rs2: e.tensor_tensor(
                    _strided(rs2.ap(), 0, [[64, 2], [1, 32]]),
                    _strided(pqx.ap(), off + 32, [[64, 2], [1, 32]]),
                    _strided(sint.ap(), 64 * tt, [[0, 2], [1, 32]]), ALU.mult),
                    reads=[pqx, sint], writes=[rs2])
                k.op("v", lambda e, pqx=pqx, tt=tt, off=off, rs2=rs2: e.tensor_tensor(
                    _strided(rs2.ap(), 32, [[64, 2], [1, 32]]),
                    _strided(pqx.ap(), off, [[64, 2], [1, 32]]),
                    _strided(sint.ap(), 64 * tt + 32, [[0, 2], [1, 32]]), ALU.mult),
                    reads=[pqx, sint], writes=[rs2])
                k.op("v", lambda e, rs=rs, rs2=rs2, rsb=rsb: e.tensor_add(
                    rsb.ap(), rs.ap(), rs2.ap()),
                    reads=[rs, rs2], writes=[rsb])
            else:
                k.op("v", lambda e, pqx=pqx, off=off, rsb=rsb: e.tensor_copy(
                    rsb.ap(), pqx.t[:, off:off + 128]),
                    reads=[pqx], writes=[rsb])
            for i in range(2):
                k.op("s", lambda e, i=i, tt=tt, ncb=ncb, rsb=rsb: e.activation(
                    nrm["dump"].t[:, 64 * i:64 * (i + 1)],
                    rsb.t[:, 64 * i:64 * (i + 1)], AF.Square,
                    accum_out=ncol[i].t[:, ncb + tt:ncb + tt + 1]),
                    reads=[rsb], writes=[nrm["dump"], ncol[i]])
            k.op("t", lambda e, j=j, rsb=rsb, pt=pt: e.transpose(
                pt.t[:, 128 * j:128 * (j + 1)], rsb.ap(), eye.ap()),
                reads=[rsb, eye], writes=[pt])
            for i in range(2):
                dst = dst2[i]
                k.op("v", lambda e, tt=tt, dst=dst, i=i, j=j, pt=pt: e.tensor_copy(
                    dst.t[0:64, 128 * tt:128 * (tt + 1)],
                    pt.t[64 * i:64 * (i + 1), 128 * j:128 * (j + 1)]),
                    reads=[pt], writes=[dst])
        for i in range(2):
            k.op("v", lambda e, tt=tt, i=i, pqx=pqx: e.tensor_copy(
                vbuf.t[:, 130 * tt + 65 * i:130 * tt + 65 * i + 64],
                pqx.t[:, 256 + 64 * i:256 + 64 * (i + 1)]),
                reads=[pqx], writes=[vbuf])
    for isq, ncb in ((True, 0), (False, 32)):
        for i in range(2):
            k.op("t", lambda e, i=i, ncb=ncb: e.transpose(
                nrm["pn"].t[0:32, 0:128], ncol[i].t[:, ncb:ncb + 32],
                g["eyef"].ap()),
                reads=[ncol[i], g["eyef"]], writes=[nrm["pn"]])
            k.op("v", lambda e, i=i: e.reduce_max(
                nrm["col2"].t[0:32, 0:1], nrm["pn"].t[0:32, 0:128], axis=AX.X),
                reads=[nrm["pn"]], writes=[nrm["col2"]])
            k.op("t", lambda e, i=i: e.transpose(
                nrm["pn2"].t[0:1, 0:32], nrm["col2"].t[0:32, 0:1],
                g["eyef"].t[0:32, 0:32]),
                reads=[nrm["col2"], g["eyef"]], writes=[nrm["pn2"]])
            rowdst = nrm["rowq"] if isq else nrm["rowk"]
            for qb in range(8):
                k.op("v", lambda e, i=i, qb=qb, rowdst=rowdst: e.tensor_reduce(
                    rowdst.t[0:1, 8 * i + qb:8 * i + qb + 1],
                    nrm["pn2"].t[0:1, 4 * qb:4 * qb + 4], AX.X, ALU.max),
                    reads=[nrm["pn2"]], writes=[rowdst])


def _attn_head(k, g, sub, layer, h, i, q_ap, k_ap, qsrc, ksrc, vbuf, attnP,
               pe, pv, prep, preps, wT, ccol, bcol2, kposc, biasq, dtile, mC, rrow):
    NQB = 8
    pending = [None]

    def flush_norm():
        if pending[0] is not None:
            pending[0]()
            pending[0] = None

    for qb in range(NQB):
        pvx = pv[qb % 2]
        if layer == 0:
            kcs = list(range(32))
            k.op("v", lambda e, h=h, qb=qb: e.tensor_scalar(
                biasq.t[:, 32 * (qb % 2):32 * (qb % 2) + 32], kposc.ap(),
                ccol.t[:, h:h + 1],
                bcol2.t[:, 8 * h + qb:8 * h + qb + 1], ALU.mult, ALU.add),
                reads=[kposc, ccol, bcol2], writes=[biasq])
        else:
            kcs = list(range(4 * qb + 4))
        first_kc, last_kc = kcs[0], kcs[-1]
        nk = len(kcs)

        def emit_qk(kc, qb=qb):
            diag = (4 * qb <= kc < 4 * qb + 4)
            lower = kc < 4 * qb
            pex = pe[kc % 4]
            KK = 70 if (layer == 0 and (lower or diag)) else 64
            k.op("t", lambda e, kc=kc, KK=KK, pex=pex: e.matmul(
                pex.ap(), k_ap(KK, kc), q_ap(KK, qb), start=True, stop=True),
                reads=[ksrc, qsrc], writes=[pex])
            if diag:
                jj = kc - 4 * qb
                if layer == 0:
                    k.op("v", lambda e, jj=jj, pex=pex: e.scalar_tensor_tensor(
                        pex.ap(), dtile[jj].ap(), ccol.t[:, h:h + 1], pex.ap(),
                        ALU.mult, ALU.add),
                        reads=[dtile[jj], ccol, pex], writes=[pex])
                else:
                    k.op("v", lambda e, jj=jj, pex=pex: e.tensor_tensor(
                        pex.ap(), pex.ap(), dtile[jj].ap(), ALU.add),
                        reads=[pex, dtile[jj]], writes=[pex])

        def emit_exp(kc, qb=qb):
            lower = kc < 4 * qb
            pex = pe[kc % 4]
            wTx = wT[kc % 4]
            if layer == 0 and lower:
                bias_ap = biasq.t[:, 32 * (qb % 2) + kc:32 * (qb % 2) + kc + 1]
                bias_rd = [biasq]
            else:
                bias_ap, bias_rd = mC.t[:, 8 * i + qb:8 * i + qb + 1], [mC]
            k.op("s", lambda e, pex=pex, wTx=wTx, bias_ap=bias_ap: e.activation(
                wTx.ap(), pex.ap(), AF.Exp, bias=bias_ap),
                reads=[pex] + bias_rd, writes=[wTx])

        def emit_pv(kc, qb=qb, pvx=pvx, first_kc=first_kc, last_kc=last_kc):
            wTx = wT[kc % 4]
            k.op("t", lambda e, kc=kc, wTx=wTx: e.matmul(
                pvx.ap(), vbuf.t[:, 130 * kc + 65 * i:130 * kc + 65 * (i + 1)],
                wTx.ap(), start=(kc == first_kc), stop=(kc == last_kc)),
                reads=[vbuf, wTx],
                writes=[pvx] if (kc == first_kc or kc == last_kc) else [],
                inc=(kc == last_kc))

        for idx in range(nk + 2):
            if idx < nk:
                emit_qk(kcs[idx])
            if idx == 3:
                flush_norm()
            if 1 <= idx < nk + 1:
                emit_exp(kcs[idx - 1])
            if idx >= 2:
                emit_pv(kcs[idx - 2])
        if nk + 2 <= 3:
            flush_norm()

        def do_norm(qb=qb, pvx=pvx):
            k.op("v", lambda e, pvx=pvx: e.reciprocal(rrow.ap(), pvx.t[64:65, :]),
                 reads=[pvx], writes=[rrow])
            k.op("t", lambda e: e.matmul(prep.ap(), g["ones_row_f"].t[0:1, 0:65],
                                         rrow.ap(), start=True, stop=True),
                 reads=[g["ones_row_f"], rrow], writes=[prep])
            k.op("v", lambda e: e.tensor_copy(preps.ap(), prep.ap()),
                 reads=[prep], writes=[preps])
            k.op("v", lambda e, qb=qb, pvx=pvx: e.tensor_tensor(
                attnP.t[64 * i:64 * i + 64,
                        4096 * (h // 2) + 512 * qb:4096 * (h // 2) + 512 * (qb + 1)],
                pvx.t[0:64, :], preps.t[0:64, :], ALU.mult),
                reads=[pvx, preps], writes=[attnP])
        pending[0] = do_norm
    flush_norm()


def _build_rope_tables(k, es, g, sub, cost, sint):
    with ExitStack() as tes:
        invf = k.sb(tes, f"s{sub}_invf", [1, 32], F32)
        invr = k.sb(tes, f"s{sub}_invr", [128, 32], F32)
        posc = k.sb(tes, f"s{sub}_posc", [128, 32], F32)
        frq = k.sb(tes, f"s{sub}_frq", [128, 512], F32)
        red = k.sb(tes, f"s{sub}_red", [128, 256], F32)
        z1 = k.sb(tes, f"s{sub}_rz1", [128, 256], I16)
        half = k.sb(tes, f"s{sub}_half", [128, 256], F32)
        sc = k.sb(tes, f"s{sub}_sc", [128, 1], F32)
        bc = k.sb(tes, f"s{sub}_bc", [128, 1], F32)
        pc = k.sb(tes, f"s{sub}_pc", [128, 1], F32)
        pbi = k.ps(tes, f"s{sub}_pbi", [128, 32])
        LN1E4_32 = float(np.log(10000.0) / 32.0)
        INV2PI = float(1.0 / (2.0 * np.pi))
        C1 = float(np.float32(6.28125))
        C2 = float(np.float32(2.0 * np.pi - 6.28125))
        k.op("g", lambda e: e.iota(invf.ap(), pattern=[[1, 32]], base=0,
                                   channel_multiplier=0,
                                   allow_small_or_imprecise_dtypes=True),
             writes=[invf])
        k.op("v", lambda e: e.memset(sc.ap(), -LN1E4_32), writes=[sc])
        k.op("s", lambda e: e.activation(invf.ap(), invf.ap(), AF.Exp,
                                         scale=sc.t[0:1, :]),
             reads=[invf, sc], writes=[invf])
        k.op("t", lambda e: e.matmul(pbi.ap(), g["ones_row_f"].ap(), invf.ap(),
                                     start=True, stop=True),
             reads=[g["ones_row_f"], invf], writes=[pbi])
        k.op("v", lambda e: e.tensor_copy(invr.ap(), pbi.ap()),
             reads=[pbi], writes=[invr])
        k.op("g", lambda e: e.iota(posc.ap(), pattern=[[128, 32]], base=0,
                                   channel_multiplier=1,
                                   allow_small_or_imprecise_dtypes=True),
             writes=[posc])
        k.op("v", lambda e: e.memset(sc.ap(), INV2PI), writes=[sc])
        for hv in range(2):
            for t_ in range(16):
                k.op("v", lambda e, hv=hv, t_=t_: e.tensor_scalar(
                    frq.t[:, 32 * t_:32 * (t_ + 1)], invr.ap(),
                    posc.t[:, 16 * hv + t_:16 * hv + t_ + 1], None, ALU.mult),
                    reads=[invr, posc], writes=[frq])
            for table, phase in ((sint, 0.0), (cost, float(np.pi / 2))):
                k.op("v", lambda e, phase=phase: e.memset(
                    bc.ap(), 16384.0 + phase * INV2PI), writes=[bc])
                k.op("v", lambda e, phase=phase: e.memset(pc.ap(), phase),
                     writes=[pc])
                for chk in range(2):
                    fap = frq.t[:, 256 * chk:256 * (chk + 1)]
                    k.op("s", lambda e, fap=fap: e.activation(
                        red.ap(), fap, AF.Identity,
                        bias=bc.ap(), scale=sc.ap()),
                        reads=[frq, sc, bc], writes=[red])
                    k.op("v", lambda e: e.tensor_copy(z1.ap(), red.ap()),
                         reads=[red], writes=[z1])
                    k.op("v", lambda e: e.scalar_tensor_tensor(
                        red.ap(), z1.ap(), 0.5, red.ap(), ALU.add, ALU.is_le),
                        reads=[z1, red], writes=[red])
                    k.op("v", lambda e: e.scalar_tensor_tensor(
                        red.ap(), red.ap(), -16384.0, z1.ap(), ALU.add, ALU.add),
                        reads=[red, z1], writes=[red])
                    k.op("s", lambda e, fap=fap: e.activation(
                        half.ap(), fap, AF.Identity, bias=pc.ap(), scale=1.0),
                        reads=[frq, pc], writes=[half])
                    k.op("v", lambda e: e.scalar_tensor_tensor(
                        half.ap(), red.ap(), -C1, half.ap(), ALU.mult, ALU.add),
                        reads=[red, half], writes=[half])
                    k.op("v", lambda e: e.scalar_tensor_tensor(
                        half.ap(), red.ap(), -C2, half.ap(), ALU.mult, ALU.add),
                        reads=[red, half], writes=[half])
                    k.op("v", lambda e: e.tensor_scalar(
                        half.ap(), half.ap(), float(np.pi - 1e-6),
                        float(-(np.pi - 1e-6)), ALU.min, ALU.max),
                        reads=[half], writes=[half])
                    k.op("s", lambda e: e.activation(half.ap(), half.ap(),
                                                     AF.Sin),
                         reads=[half], writes=[half])
                    k.op("v", lambda e, table=table, hv=hv, chk=chk:
                         e.tensor_copy(
                             _strided(table.ap(), 1024 * hv + 512 * chk,
                                      [[64, 8], [1, 32]]),
                             _strided(half.ap(), 0, [[32, 8], [1, 32]])),
                         reads=[half], writes=[table])
                    k.op("v", lambda e, table=table, hv=hv, chk=chk:
                         e.tensor_copy(
                             _strided(table.ap(), 1024 * hv + 512 * chk + 32,
                                      [[64, 8], [1, 32]]),
                             _strided(half.ap(), 0, [[32, 8], [1, 32]])),
                         reads=[half], writes=[table])
        k.op("v", lambda e: e.tensor_scalar(
            _strided(sint.ap(), 0, [[64, 32], [1, 32]]),
            _strided(sint.ap(), 0, [[64, 32], [1, 32]]), -1.0, None, ALU.mult),
            reads=[sint], writes=[sint])
        k.join()


_NC_CACHE = {}


def _get_nc(nsub):
    if nsub not in _NC_CACHE:
        _NC_CACHE[nsub] = build_program(nsub)
    return _NC_CACHE[nsub]


def kernel(L0_attn_x, L0_attn_qkv_w, L0_attn_proj_w, L0_attn_proj_b,
           L0_attn_slopes, L0_ffn_w1, L0_ffn_conv_w, L0_ffn_conv_b, L0_ffn_w2,
           L1_attn_w_q, L1_attn_w_k, L1_attn_w_v, L1_attn_w_o,
           L1_ffn_router_w, L1_ffn_router_b, L1_ffn_expert_w, L1_ffn_expert_b):
    nsub = N_SUBLAYERS
    nc = _get_nc(nsub)

    x = np.ascontiguousarray(np.asarray(L0_attn_x, dtype=np.float32))
    w1t = np.ascontiguousarray(np.asarray(L0_ffn_w1, np.float32).T)
    w2t = np.ascontiguousarray(np.asarray(L0_ffn_w2, np.float32).T)
    convw = np.ascontiguousarray(np.asarray(L0_ffn_conv_w, np.float32)
                                 .reshape(NCT, 128, 3).transpose(1, 0, 2).reshape(128, 3 * NCT))
    convb = np.ascontiguousarray(np.asarray(L0_ffn_conv_b, np.float32)
                                 .reshape(NCT, 128).T)
    qkv_w = np.asarray(L0_attn_qkv_w, np.float32)
    proj_wt = np.ascontiguousarray(np.asarray(L0_attn_proj_w, np.float32).T)
    proj_b = np.asarray(L0_attn_proj_b, np.float32)
    slopes = np.asarray(L0_attn_slopes, np.float32)
    wq_l1 = np.asarray(L1_attn_w_q, np.float32)
    wk_l1 = np.asarray(L1_attn_w_k, np.float32)
    wv_l1 = np.asarray(L1_attn_w_v, np.float32)
    wo_l1 = np.asarray(L1_attn_w_o, np.float32)
    router_w = np.asarray(L1_ffn_router_w, np.float32)
    router_b = np.asarray(L1_ffn_router_b, np.float32)
    expert_w = np.asarray(L1_ffn_expert_w, np.float32)
    expert_b = np.asarray(L1_ffn_expert_b, np.float32)

    routerw_s = np.ascontiguousarray(
        np.concatenate([router_w, router_b[None, :]], axis=0))
    expertw_s = np.ascontiguousarray(
        np.concatenate([expert_w, expert_b[:, None, :]], axis=1))

    in_maps = []
    for c in range(8):
        b, hf = c // 2, c % 2
        r0 = hf * TOK
        heads = range(8 * hf, 8 * hf + 8)
        halo = np.zeros((2, C), np.float32)
        if r0 - 1 >= 0:
            halo[0] = x[b, r0 - 1]
        if r0 + TOK < N:
            halo[1] = x[b, r0 + TOK]
        qsl = np.concatenate([qkv_w[64 * h:64 * h + 64] for h in heads], 0)
        ksl = np.concatenate([qkv_w[1024 + 64 * h:1024 + 64 * h + 64] for h in heads], 0)
        vsl = np.concatenate([qkv_w[2048 + 64 * h:2048 + 64 * h + 64] for h in heads], 0)
        qkvwt_s = np.ascontiguousarray(np.concatenate([qsl, ksl, vsl], 0).T)
        def stage_proj(wt_in, bias):
            blocks = [wt_in[64 * hh:64 * hh + 64] for hh in heads]
            blocks.append(bias[None, :])
            return np.ascontiguousarray(np.concatenate(blocks, 0))
        projwt_s = stage_proj(proj_wt, proj_b if hf == 0 else np.zeros_like(proj_b))
        q1 = np.concatenate([wq_l1[h] for h in heads], 1)
        k1 = np.concatenate([wk_l1[h] for h in heads], 1)
        v1 = np.concatenate([wv_l1[h] for h in heads], 1)
        l1qkvt_s = np.ascontiguousarray(np.concatenate([q1, k1, v1], 1))
        l1wot_s = stage_proj(wo_l1, np.zeros(C, np.float32))
        metav = np.zeros((1, 8), np.float32)
        metav[0, 0] = 1.0 if hf == 0 else 0.0
        metav[0, 1] = 1.0 if hf == 1 else 0.0
        in_maps.append({
            "x_own": np.ascontiguousarray(x[b, r0:r0 + TOK]),
            "x_halo": halo,
            "w1t": w1t, "w2t": w2t, "convw": convw, "convb": convb,
            "qkvwt": qkvwt_s, "projwt": projwt_s,
            "slopes8": np.ascontiguousarray(slopes[8 * hf:8 * hf + 8][None, :]),
            "l1qkvt": l1qkvt_s, "l1wot": l1wot_s,
            "routerw": routerw_s, "expertw": expertw_s,
            "meta": metav,
        })

    res = run_bass_kernel_spmd(nc, in_maps, core_ids=list(range(8)))
    out = np.empty((B, N, C), np.float32)
    for c in range(8):
        b, hf = c // 2, c % 2
        out[b, hf * TOK:(hf + 1) * TOK] = res.results[c]["out"]
    return out
